# version 16
# baseline (speedup 1.0000x reference)
import numpy as np
import concourse.bass as bass
import concourse.mybir as mybir
from concourse.bass_utils import run_bass_kernel_spmd
from concourse.bass import IndirectOffsetOnAxis

F32 = mybir.dt.float32
BF16 = mybir.dt.bfloat16
U32 = mybir.dt.uint32
U16 = mybir.dt.uint16
AF = mybir.ActivationFunctionType
ALU = mybir.AluOpType
AX = mybir.AxisListType

NB = 2
T = 2048
D = 1024
H = 8
HD = 256
LR = 64
FF = 4096
NE = 8
KS = 512
XGW = 1048
NBIS = 38

OOBF = 1.0e9


class Sems:
    def __init__(self, nc, stack):
        self.h = {}
        self.n = {}
        self.lastw = {}
        self.nc = nc
        self.stack = stack

    def new(self, name):
        self.h[name] = self.stack.enter_context(self.nc.semaphore(name))
        self.n[name] = 0

    def inc(self, inst, name, k=1):
        inst.then_inc(self.h[name], k)
        self.n[name] += k
        return (name, self.n[name])

    def cur(self, name):
        return (name, self.n[name])

    def reserve(self, name, k=1):
        self.n[name] += k
        return (name, self.n[name])

    def attach(self, inst, name, k=1):
        inst.then_inc(self.h[name], k)

    def wait(self, eng, ename, tok):
        if tok is None:
            return
        name, v = tok
        if v <= 0:
            return
        key = (ename, name)
        if self.lastw.get(key, -1) >= v:
            return
        eng.wait_ge(self.h[name], v)
        self.lastw[key] = v


def build(debug=False):
    nc = bass.Bass("TRN2", target_bir_lowering=False)

    x_d = nc.dram_tensor("x", [NB, T, D], F32, kind="ExternalInput").ap()
    wqT_d = nc.dram_tensor("wqT", [D, H * HD], F32, kind="ExternalInput").ap()
    wkvT_d = nc.dram_tensor("wkvT", [D, LR], F32, kind="ExternalInput").ap()
    wkT_d = nc.dram_tensor("wkT", [LR, H * HD], F32, kind="ExternalInput").ap()
    wvT_d = nc.dram_tensor("wvT", [LR, H * HD], F32, kind="ExternalInput").ap()
    woT_d = nc.dram_tensor("woT", [H * HD, D], F32, kind="ExternalInput").ap()
    gwT_d = nc.dram_tensor("gwT", [D, NE], F32, kind="ExternalInput").ap()
    w1_d = nc.dram_tensor("w1", [NE, D, FF], F32, kind="ExternalInput").ap()
    w2_d = nc.dram_tensor("w2", [NE, FF, D], F32, kind="ExternalInput").ap()
    id_d = nc.dram_tensor("ident", [128, 128], F32, kind="ExternalInput").ap()

    out_d = nc.dram_tensor("out", [NB, T, D], F32, kind="ExternalOutput").ap()

    
    xgd_d = nc.dram_tensor("xgd", [NB, T, XGW], U16, kind="Internal").ap()
    x1sp_d = nc.dram_tensor("x1sp", [NB, T, D], F32, kind="Internal").ap()
    ffna_d = [nc.dram_tensor(f"ffna{b}", [T, D], F32, kind="Internal").ap()
              for b in range(NB)]
    w1bf_d = nc.dram_tensor("w1bf", [NE, D, FF], BF16, kind="Internal").ap()
    w2bf_d = nc.dram_tensor("w2bf", [NE, FF, D], BF16, kind="Internal").ap()
    xin_d = [nc.dram_tensor(f"xin{s}", [KS, XGW], U16, kind="Internal").ap()
             for s in range(4)]

    from contextlib import ExitStack
    st = ExitStack()
    S = Sems(nc, st)
    for s in ["sT", "sV", "sS", "sG", "dY", "dY2", "dY20", "dY21", "dYx0", "dYx1", "dYx2", "dYx3", "dYw0", "dYw1", "dYv0", "dYv1", "dZ", "dGw", "dGx", "dGc", "dGs0", "dGs1", "dGs2", "dGs3", "dGf", "dO"]:
        S.new(s)

    def sb(name, shape, dtype):
        cm = nc.sbuf_tensor(name, shape, dtype)
        t = cm.__enter__()
        return t, cm

    def psum(name, shape, dtype):
        cm = nc.psum_tensor(name, shape, dtype)
        t = cm.__enter__()
        return t, cm

    wkv_sb, _ = sb("wkv_sb", [128, 8, LR], BF16)
    wk_sb, _ = sb("wk_sb", [64, H * HD], BF16)
    wv_sb, _ = sb("wv_sb", [64, H * HD], BF16)
    gw_sb, _ = sb("gw_sb", [128, 8, NE], BF16)
    idb_sb, _ = sb("idb_sb", [128, 128], BF16)
    idf_sb, _ = sb("idf_sb", [128, 128], F32)
    zsq_sb, _ = sb("zsq_sb", [128, D], F32)
    stat_sb, _ = sb("stat_sb", [128, 16], F32)
    rc_sb, _ = sb("rc_sb", [128, 2], F32)
    obf_sb, _ = sb("obf_sb", [128, 2, HD], BF16)
    P_all, _ = sb("P_all", [64, T], F32)
    probs_sb, _ = sb("probs_sb", [128, 16, NE], F32)
    pexp_sb, _ = sb("pexp_sb", [128, 2, NE], F32)
    tid_sb, _ = sb("tid_sb", [128, 16], U32)
    bis_sb, _ = sb("bis_sb", [64, 8], F32)

    psE_t, _ = psum("psE", [128, 2048], F32)
    psE = [psE_t[:, q * 512:(q + 1) * 512] for q in range(4)]
    pb4_t, _ = psum("pb4", [128, 512], F32)
    pb5_t, _ = psum("pb5", [128, 512], F32)
    pb45 = [pb4_t, pb5_t]
    psT = []
    for i in range(2):
        t, _ = psum(f"psT{i}", [128, 1024], BF16)
        psT.append(t)

    bank_free = [None] * 4
    pv_free = [None] * 2
    psT_free = [None] * 2

    with nc.Block() as blk:
        def T_(fn):
            blk.tensor(fn)

        def V_(fn):
            pre = S.cur("sV")

            def w(v):
                S.wait(v, "v", pre)
                fn(v)

            blk.vector(w)

        def C_(fn):
            pre = S.cur("sS")

            def w(c):
                S.wait(c, "c", pre)
                fn(c)

            blk.scalar(w)

        def G_(fn):
            blk.gpsimd(fn)

        def Y_(fn):
            blk.sync(fn)

        def TB_(fn):
            blk.tensor(fn)

        def VB_(fn):
            blk.vector(fn)

        def CB_(fn):
            blk.scalar(fn)

        ENAME = {}

        def dma(eng, ename, sem, out, in_, **kw):
            S.wait(eng, ename, S.cur(sem))
            inst = eng.dma_start(out=out, in_=in_, **kw)
            return S.inc(inst, sem, 16)

        def g_dma(g, out, in_, sem, **kw):
            return dma(g, "g", sem, out, in_, **kw)

        bc_reg = [None]

        def setup(g):
            bc_reg[0] = g.alloc_register("bc")
            g.reg_mov(bc_reg[0], KS - 1)
            g_dma(g, wkv_sb[:, :, :], wkvT_d.rearrange("(c p) l -> p c l", p=128), "dGw")
            g_dma(g, wk_sb[:, :], wkT_d[:, :], "dGw")
            g_dma(g, wv_sb[:, :], wvT_d[:, :], "dGw")
            g_dma(g, gw_sb[:, :, :], gwT_d.rearrange("(c p) e -> p c e", p=128), "dGw")
            g_dma(g, idb_sb[:, :], id_d[:, :], "dGw")
            g_dma(g, idf_sb[:, :], id_d[:, :], "dGw")

        G_(setup)
        tok_smallw = S.cur("dGw")

        def setup_g2(g):
            inst = g.iota(tid_sb[:, :], pattern=[[128, 16]], base=0,
                          channel_multiplier=1)
            S.inc(inst, "sG", 1)

        G_(setup_g2)
        tok_tid = S.cur("sG")

        def setup_v(v):
            S.inc(v.memset(zsq_sb[:, :], 0.0), "sV")
            S.inc(v.memset(stat_sb[:, 7:8], 1e-5), "sV")
            S.inc(v.memset(P_all[:, :], 0.0), "sV")

        V_(setup_v)
        tok_zero = S.cur("sV")

        def setup_z(z):
            S.wait(z, "z", tok_zero)
            zb = zsq_sb[:, 0:48].bitcast(U16).rearrange("p (t m) -> p t m", m=6)
            for b in range(NB):
                dma(z, "z", "dZ",
                    xgd_d[b, :, 1042:1048].rearrange("(t p) m -> p t m", p=128),
                    zb)

        C_(setup_z)
        tok_ffna_zero = S.cur("dZ")

        wuni_sb, wuni_cm = sb("wuni_sb", [128, 16384], BF16)
        wq_view = wuni_sb.ap().rearrange("p (c f) -> p c f", c=8)
        wo_view = wuni_sb.ap().rearrange("p (c f) -> p c f", c=16)
        xT_sb, xT_cm = sb("xT_sb", [128, 8, T], BF16)
        qT_sb, qT_cm = sb("qT_sb", [128, 2, T], BF16)
        kT_sb, kT_cm = sb("kT_sb", [128, 2, T], BF16)
        kvl_sb, kvl_cm = sb("kvl_sb", [64, T], BF16)
        v_sb, v_cm = sb("v_sb", [128, 16, HD + 1], BF16)
        E_sb, E_cm = sb("E_sb", [128, 1, 16, 512], BF16)
        oT_sb, oT_cm = sb("oT_sb", [128, 16, T], BF16)
        x0r_sb, x0r_cm = sb("x0r_sb", [128, 1, D], F32)
        xbf_sb, xbf_cm = sb("xbf_sb", [128, 2, D], BF16)
        x1r_sb, x1r_cm = sb("x1r_sb", [128, 1, D], F32)

        def setup_v2(v):
            S.inc(v.memset(v_sb[:, :, HD:HD + 1], 1.0), "sV")

        V_(setup_v2)
        tok_vones = S.cur("sV")

        def wq_load0(g):
            g_dma(g, wq_view, wqT_d.rearrange("(c p) f -> p c f", p=128), "dGw")

        G_(wq_load0)
        tok_wq = [S.cur("dGw"), None]
        tok_wo = [None, None]

        wcast_tok = [None] * (2 * NE)
        next_wcast = [0]

        def emit_wcast(g=None):
            i = next_wcast[0]
            if i >= 2 * NE:
                return
            next_wcast[0] = i + 1
            e, which = i // 2, i % 2

            def f(g, e=e, which=which):
                if which == 0:
                    g_dma(g, w1bf_d[e].rearrange("d f -> d f"), w1_d[e], "dGc")
                else:
                    g_dma(g, w2bf_d[e].rearrange("f d -> f d"), w2_d[e], "dGc")

            G_(f)
            wcast_tok[i] = S.cur("dGc")

        tok_xT = [None, None]
        tok_x1sp = [None, None]
        tok_xgd_x1 = [None, None]
        tok_xgd_meta = [None, None]
        tok_x1T = [None, None]
        tok_Pall = [None, None]

        E_free = [None, None]
        obf_free = [None, None]
        qkv_free = None
        wq_used = [None, None]
        gate_done_tok = [None, None]
        x0r_free = [None, None]
        xbf_war = [None, None]

        xT_psT_par = [0]

        def emit_xT_tile(src_f32_tile_emitter, dst_col, xbf_slot, extra_war=None):
            par = xT_psT_par[0]
            xT_psT_par[0] ^= 1
            ctok = S.reserve("sV")
            trq = S.reserve("sT")
            evq = S.reserve("sV")
            psT_at = psT_free[par]
            psT_free[par] = evq

            def vcast(v):
                src_f32_tile_emitter(v)

            def ftr(t, psT_at=psT_at, ctok=ctok, par=par, xbf_slot=xbf_slot):
                S.wait(t, "t", tok_smallw)
                S.wait(t, "t", ctok)
                S.wait(t, "t", psT_at)
                for c in range(8):
                    inst = t.transpose(psT[par][:, c * 128:(c + 1) * 128],
                                       xbf_sb[:, xbf_slot, c * 128:(c + 1) * 128],
                                       idb_sb[:, :])
                S.attach(inst, "sT")

            def fev(v, trq=trq, dst_col=dst_col, extra_war=extra_war, par=par):
                if extra_war is not None:
                    S.wait(v, "v", extra_war)
                S.wait(v, "v", trq)
                inst = v.tensor_copy(
                    xT_sb[:, :, dst_col:dst_col + 128],
                    psT[par][:, :].rearrange("p (a q) -> p a q", a=8))
                S.attach(inst, "sV")

            VB_(vcast)
            TB_(ftr)
            VB_(fev)
            return ctok, trq, evq


        def emit_var_eps(v, base):
            va = stat_sb[:, base + 3:base + 4]
            i = v.tensor_scalar(out=va, in0=va, scalar1=1e-5, scalar2=None,
                                op0=ALU.add)
            S.inc(i, "sV", 1)
            S.wait(v, "v", S.cur("sV"))
            i = v.reciprocal(stat_sb[:, base + 4:base + 5], va)
            S.inc(i, "sV", 1)

        def emit_rsqrt_sec(base, vtok):
            def rs(c, base=base, vtok=vtok):
                S.wait(c, "c", vtok)
                i = c.activation(out=stat_sb[:, base + 5:base + 6],
                                 in_=stat_sb[:, base + 4:base + 5],
                                 func=AF.Sqrt)
                S.inc(i, "sS", 1)

            C_(rs)
            return S.cur("sS")

        for b in range(NB):
            for tp in range(16):
                rp = tp % 2

                def x0_load(y, b=b, tp=tp):
                    S.wait(y, "y", x0r_free[0])
                    dma(y, "y", "dY2", x0r_sb[:, 0, :],
                        x_d[b, tp * 128:(tp + 1) * 128, :])

                Y_(x0_load)
                ldtok = S.cur("dY2")

                def cast_em(v, ldtok=ldtok, xw=xbf_war[rp], rp=rp):
                    S.wait(v, "v", ldtok)
                    if xw is not None:
                        S.wait(v, "v", xw)
                    inst = v.tensor_copy(xbf_sb[:, rp, :], x0r_sb[:, 0, :])
                    S.attach(inst, "sV")

                ctok, trq, evq = emit_xT_tile(
                    cast_em, tp * 128, rp,
                    gate_done_tok[0] if b == 1 else None)
                xbf_war[rp] = trq
                x0r_free[0] = ctok
            tok_xT[b] = evq

            if b == 1:
                def wq_load1(g):
                    g.wait_ge(S.h["sT"], gate_done_tok[0][1])
                    g_dma(g, wq_view,
                          wqT_d.rearrange("(c p) f -> p c f", p=128), "dGw")

                G_(wq_load1)
                tok_wq[1] = S.cur("dGw")


            for h in range(H):
                jobs = []
                if h == 0:
                    jobs += [("kvl", qc) for qc in range(4)]
                jobs += [("q", (m, qc)) for m in range(2) for qc in range(4)]
                jobs += [("k", (m, qc)) for m in range(2) for qc in range(4)]
                jobs += [("v", kp) for kp in range(16)]
                pmm = [S.reserve("sT") for _ in jobs]
                pev = [S.reserve("sV") for _ in jobs]
                ent_at = []
                ent = list(bank_free)
                for i in range(len(jobs)):
                    ent_at.append(ent[i % 4])
                    ent[i % 4] = pev[i]
                bank_free = ent
                nkvl = 4 if h == 0 else 0
                if h == 0:
                    kvl_tok = pev[3]
                v_tok = pev[-1]

                def proj_sec(t, b=b, h=h, jobs=tuple(jobs), ent_at=tuple(ent_at),
                             qf=qkv_free, kvt=kvl_tok):
                    S.wait(t, "t", tok_smallw)
                    S.wait(t, "t", tok_wq[b])
                    S.wait(t, "t", tok_xT[b])
                    if qf is not None:
                        S.wait(t, "t", qf)
                    for i, (kind, pr) in enumerate(jobs):
                        S.wait(t, "t", ent_at[i])
                        bk = i % 4
                        if kind == "kvl":
                            qc = pr
                            for k in range(8):
                                inst = t.matmul(psE[bk][0:64, :], wkv_sb[:, k, :],
                                                xT_sb[:, k, qc * 512:(qc + 1) * 512],
                                                start=(k == 0), stop=(k == 7))
                        elif kind == "q":
                            m, qc = pr
                            for k in range(8):
                                inst = t.matmul(
                                    psE[bk][:, :],
                                    wq_view[:, k, h * 256 + m * 128: h * 256 + (m + 1) * 128],
                                    xT_sb[:, k, qc * 512:(qc + 1) * 512],
                                    start=(k == 0), stop=(k == 7))
                        elif kind == "k":
                            m, qc = pr
                            S.wait(t, "t", kvt)
                            inst = t.matmul(
                                psE[bk][:, :],
                                wk_sb[0:64, h * 256 + m * 128: h * 256 + (m + 1) * 128],
                                kvl_sb[0:64, qc * 512:(qc + 1) * 512],
                                start=True, stop=True)
                        else:
                            kp = pr
                            S.wait(t, "t", kvt)
                            inst = t.matmul(psE[bk][:, 0:HD],
                                            kvl_sb[0:64, kp * 128:(kp + 1) * 128],
                                            wv_sb[0:64, h * 256:(h + 1) * 256],
                                            start=True, stop=True)
                        S.attach(inst, "sT")

                TB_(proj_sec)

                def proj_ev(v, h=h, jobs=tuple(jobs), pmm=tuple(pmm),
                            pev=tuple(pev)):
                    S.wait(v, "v", tok_vones)
                    for i, (kind, pr) in enumerate(jobs):
                        S.wait(v, "v", pmm[i])
                        bk = i % 4
                        if kind == "kvl":
                            qc = pr
                            inst = v.tensor_copy(
                                kvl_sb[0:64, qc * 512:(qc + 1) * 512],
                                psE[bk][0:64, :])
                        elif kind == "q":
                            m, qc = pr
                            inst = v.tensor_copy(
                                qT_sb[:, m, qc * 512:(qc + 1) * 512], psE[bk][:, :])
                        elif kind == "k":
                            m, qc = pr
                            inst = v.tensor_copy(
                                kT_sb[:, m, qc * 512:(qc + 1) * 512], psE[bk][:, :])
                        else:
                            kp = pr
                            inst = v.tensor_copy(v_sb[:, kp, 0:HD], psE[bk][:, 0:HD])
                        S.attach(inst, "sV")

                VB_(proj_ev)
                wq_used[b] = pmm[nkvl + 7]

                for qc in range(4):
                    smm = [S.reserve("sT") for _ in range(16)]
                    sexp = [S.reserve("sS") for _ in range(16)]
                    t_order = ["pv0", "pv1", "tr0", "pv2", "tr1", "pv3", "tr2", "tr3"]
                    ptk = {nm: S.reserve("sT") for nm in t_order}
                    v_order = ["nrm0", "nrm1", "ocp0", "nrm2", "ocp1", "nrm3",
                               "ocp2", "ocp3"]
                    vtk = {}
                    for nm in v_order:
                        vtk[nm] = S.reserve("sV", 2 if nm.startswith("nrm") else 1)
                    sent = list(bank_free)
                    sent_at = []
                    for kp in range(16):
                        sent_at.append(sent[kp % 4])
                        sent[kp % 4] = sexp[kp]
                    bank_free = sent
                    pv_ent = [pv_free[0], pv_free[1]]
                    pv_at = []
                    for qp in range(4):
                        pv_at.append(pv_ent[qp % 2])
                        pv_ent[qp % 2] = vtk[f"nrm{qp}"]
                    pv_free[0], pv_free[1] = pv_ent[0], pv_ent[1]
                    psT_at = [psT_free[0], psT_free[1]]
                    obf_at = [obf_free[0], obf_free[1]]
                    for qp in range(4):
                        pp = qp % 2
                        psT_at.append(psT_free[pp])
                        obf_at.append(obf_free[pp])
                        psT_free[pp] = vtk[f"ocp{qp}"]
                        obf_free[pp] = ptk[f"tr{qp}"]
                    psT_at = psT_at[2:]
                    obf_at = obf_at[2:]

                    def sc_sec(t, h=h, qc=qc, smm=tuple(smm), sexp=tuple(sexp),
                               sent_at=tuple(sent_at), pv_at=tuple(pv_at),
                               psT_at=tuple(psT_at), ptk=dict(ptk), vtk=dict(vtk),
                               efree=E_free[0], v_tok=v_tok):
                        S.wait(t, "t", efree)
                        for kp in range(16):
                            S.wait(t, "t", sent_at[kp])
                            bk = kp % 4
                            for m in range(2):
                                inst = t.matmul(psE[bk][:, :],
                                                kT_sb[:, m, kp * 128:(kp + 1) * 128],
                                                qT_sb[:, m, qc * 512:(qc + 1) * 512],
                                                start=(m == 0), stop=(m == 1))
                            S.attach(inst, "sT")

                        def emit_pv(qp):
                            bk = qp % 2
                            S.wait(t, "t", sexp[15])
                            S.wait(t, "t", v_tok)
                            S.wait(t, "t", pv_at[qp])
                            for kp in range(16):
                                inst = t.matmul(
                                    pb45[bk][:, 0:HD + 1],
                                    E_sb[:, 0, kp, qp * 128:(qp + 1) * 128],
                                    v_sb[:, kp, :],
                                    start=(kp == 0), stop=(kp == 15))
                            S.attach(inst, "sT")

                        def emit_tr(qp):
                            par = qp % 2
                            S.wait(t, "t", vtk[f"nrm{qp}"])
                            S.wait(t, "t", psT_at[qp])
                            t.transpose(psT[par][:, 0:128], obf_sb[:, par, 0:128],
                                        idb_sb[:, :])
                            inst = t.transpose(psT[par][:, 128:256],
                                               obf_sb[:, par, 128:256],
                                               idb_sb[:, :])
                            S.attach(inst, "sT")

                        emit_pv(0)
                        emit_pv(1)
                        emit_tr(0)
                        emit_pv(2)
                        emit_tr(1)
                        emit_pv(3)
                        emit_tr(2)
                        emit_tr(3)

                    TB_(sc_sec)

                    def exp_sec(c, qc=qc, smm=tuple(smm)):
                        for kp in range(16):
                            S.wait(c, "c", smm[kp])
                            inst = c.activation(out=E_sb[:, 0, kp, :],
                                                in_=psE[kp % 4][:, :],
                                                func=AF.Exp, scale=1.0 / 16.0)
                            S.attach(inst, "sS")

                    CB_(exp_sec)

                    def pvv_sec(v, h=h, qc=qc, ptk=dict(ptk), vtk=dict(vtk),
                                obf_at=tuple(obf_at)):
                        def emit_nrm(qp):
                            bk = qp % 2
                            par = qp % 2
                            S.wait(v, "v", ptk[f"pv{qp}"])
                            S.wait(v, "v", obf_at[qp])
                            i1 = v.reciprocal(rc_sb[:, par:par + 1],
                                              pb45[bk][:, HD:HD + 1])
                            S.attach(i1, "sV")
                            v.wait_ge(S.h["sV"], vtk[f"nrm{qp}"][1] - 1)
                            i2 = v.tensor_scalar(out=obf_sb[:, par, :],
                                                 in0=pb45[bk][:, 0:HD],
                                                 scalar1=rc_sb[:, par:par + 1],
                                                 scalar2=None, op0=ALU.mult)
                            S.attach(i2, "sV")

                        def emit_ocp(qp):
                            par = qp % 2
                            qglob = qc * 512 + qp * 128
                            S.wait(v, "v", ptk[f"tr{qp}"])
                            inst = v.tensor_copy(
                                oT_sb[:, 2 * h:2 * h + 2, qglob:qglob + 128],
                                psT[par][:, 0:256].rearrange("p (a q) -> p a q", a=2))
                            S.attach(inst, "sV")

                        emit_nrm(0)
                        emit_nrm(1)
                        emit_ocp(0)
                        emit_nrm(2)
                        emit_ocp(1)
                        emit_nrm(3)
                        emit_ocp(2)
                        emit_ocp(3)

                    VB_(pvv_sec)
                    E_free[0] = ptk["pv3"]
                qkv_free = ptk["tr3"]
                if b == 0 and h >= 2:
                    emit_wcast()
                elif b == 1:
                    emit_wcast()
                    if h < 2:
                        emit_wcast()
            oT_ready = S.cur("sV")

            def wo_load(g, b=b):
                g.wait_ge(S.h["sT"], wq_used[b][1])
                g_dma(g, wo_view, woT_d.rearrange("(c p) f -> p c f", p=128), "dGw")

            G_(wo_load)
            tok_wo[b] = S.cur("dGw")

            x1r_free = [None]
            x1r_free2 = None
            for qp in range(16):
                rpar = 0

                def x0_pref(y, b=b, qp=qp):
                    S.wait(y, "y", x0r_free[0])
                    dma(y, "y", "dY2", x0r_sb[:, 0, :],
                        x_d[b, qp * 128:(qp + 1) * 128, :])

                Y_(x0_pref)
                x0tok = S.cur("dY2")

                mmtoks = []
                for dc in range(2):
                    bk = dc

                    def wo_mm(t, b=b, qp=qp, dc=dc, bk=bk, oT_ready=oT_ready):
                        S.wait(t, "t", oT_ready)
                        S.wait(t, "t", tok_wo[b])
                        S.wait(t, "t", pv_free[bk])
                        for kc in range(16):
                            inst = t.matmul(pb45[bk][:, :],
                                            oT_sb[:, kc, qp * 128:(qp + 1) * 128],
                                            wo_view[:, kc, dc * 512:(dc + 1) * 512],
                                            start=(kc == 0), stop=(kc == 15))
                        S.inc(inst, "sT", 1)

                    T_(wo_mm)
                    mmtoks.append(S.cur("sT"))

                def resid(v, qp=qp, rpar=rpar, x0tok=x0tok, mmtoks=tuple(mmtoks)):
                    S.wait(v, "v", x0tok)
                    S.wait(v, "v", x1r_free[rpar])
                    S.wait(v, "v", x1r_free2)
                    for dc in range(2):
                        S.wait(v, "v", mmtoks[dc])
                        inst = v.tensor_tensor(out=x1r_sb[:, rpar, dc * 512:(dc + 1) * 512],
                                               in0=pb45[dc][:, :],
                                               in1=x0r_sb[:, 0, dc * 512:(dc + 1) * 512],
                                               op=ALU.add)
                        S.inc(inst, "sV", 1)

                V_(resid)
                pv_free[0] = S.cur("sV")
                pv_free[1] = S.cur("sV")
                x0r_free[0] = S.cur("sV")
                rtok = S.cur("sV")

                base = 1 + 7 * (qp % 2)

                def ln1(v, rpar=rpar, rtok=rtok, base=base):
                    S.wait(v, "v", rtok)
                    S.wait(v, "v", tok_ffna_zero)
                    tile = x1r_sb[:, rpar, :]
                    i = v.tensor_scalar(out=zsq_sb[:, :], in0=tile, scalar1=1.0 / D,
                                        scalar2=0.0, op0=ALU.mult, op1=ALU.add,
                                        accum_out=stat_sb[:, base:base + 1])
                    S.inc(i, "sV", 1)
                    S.wait(v, "v", S.cur("sV"))
                    i = v.scalar_tensor_tensor(out=zsq_sb[:, :], in0=tile,
                                               scalar=1.0 / D, in1=tile,
                                               op0=ALU.mult, op1=ALU.mult,
                                               accum_out=stat_sb[:, base + 1:base + 2])
                    S.inc(i, "sV", 1)
                    S.wait(v, "v", S.cur("sV"))
                    i = v.tensor_tensor(out=stat_sb[:, base + 2:base + 3],
                                        in0=stat_sb[:, base:base + 1],
                                        in1=stat_sb[:, base:base + 1], op=ALU.mult)
                    S.inc(i, "sV", 1)
                    S.wait(v, "v", S.cur("sV"))
                    i = v.tensor_tensor(out=stat_sb[:, base + 3:base + 4],
                                        in0=stat_sb[:, base + 1:base + 2],
                                        in1=stat_sb[:, base + 2:base + 3],
                                        op=ALU.subtract)
                    S.inc(i, "sV", 1)
                    S.wait(v, "v", S.cur("sV"))
                    emit_var_eps(v, base)

                V_(ln1)
                vtok = S.cur("sV")
                stok = emit_rsqrt_sec(base, vtok)

                def ln1_ap(v, rpar=rpar, base=base, stok=stok):
                    S.wait(v, "v", stok)
                    tile = x1r_sb[:, rpar, :]
                    i = v.tensor_scalar(out=tile, in0=tile,
                                        scalar1=stat_sb[:, base:base + 1],
                                        scalar2=stat_sb[:, base + 5:base + 6],
                                        op0=ALU.subtract, op1=ALU.mult)
                    S.inc(i, "sV", 1)

                V_(ln1_ap)
                atok = S.cur("sV")

                def x1_store(y, b=b, qp=qp, rpar=rpar, atok=atok):
                    S.wait(y, "y", atok)
                    dma(y, "y", "dY", x1sp_d[b, qp * 128:(qp + 1) * 128, :],
                        x1r_sb[:, rpar, :])

                Y_(x1_store)
                store_tok = S.cur("dY")

                xrp = qp % 2

                def cast1_em(v, atok=atok, xw=xbf_war[xrp], xrp=xrp):
                    S.wait(v, "v", atok)
                    if xw is not None:
                        S.wait(v, "v", xw)
                    inst = v.tensor_copy(xbf_sb[:, xrp, :], x1r_sb[:, 0, :])
                    S.attach(inst, "sV")

                ctok, trq, evq = emit_xT_tile(cast1_em, qp * 128, xrp)
                xbf_war[xrp] = trq
                x1r_free[rpar] = store_tok
                x1r_free2 = ctok
            tok_x1sp[b] = S.cur("dY")
            tok_x1T[b] = evq



            def xg_cast(g, b=b):
                g.wait_ge(S.h["dY"], tok_x1sp[b][1])
                g_dma(g, xgd_d[b, :, 0:1024].bitcast(BF16), x1sp_d[b, :, :], "dGx")

            G_(xg_cast)
            tok_xgd_x1[b] = S.cur("dGx")

            gmm = [S.reserve("sT") for _ in range(16)]
            nmx_t = [None] * 16
            fin_t = [None] * 16
            nmx_t[0] = S.reserve("sV")
            for tp in range(1, 16):
                nmx_t[tp] = S.reserve("sV")
                fin_t[tp - 1] = S.reserve("sV", 2)
            fin_t[15] = S.reserve("sV", 2)
            gex = [S.reserve("sS") for _ in range(16)]
            ptr_t = [S.reserve("sT") for _ in range(16)]
            pcp_t = [S.reserve("sV") for _ in range(16)]
            gbank_at = list(bank_free) + [gex[tp - 4] for tp in range(4, 16)]
            tbank_at = [pv_free[0], pv_free[1]] + [pcp_t[tp - 2] for tp in range(2, 16)]
            bank_free = [gex[12], gex[13], gex[14], gex[15]]
            pv_free[0], pv_free[1] = pcp_t[14], pcp_t[15]

            def gate_t(t, b=b, gbank_at=tuple(gbank_at), tbank_at=tuple(tbank_at),
                       fin_t=tuple(fin_t)):
                S.wait(t, "t", tok_x1T[b])
                for tp in range(16):
                    S.wait(t, "t", gbank_at[tp])
                    for k in range(8):
                        inst = t.matmul(psE[tp % 4][:, 0:NE],
                                        xT_sb[:, k, tp * 128:(tp + 1) * 128],
                                        gw_sb[:, k, :],
                                        start=(k == 0), stop=(k == 7))
                    S.attach(inst, "sT")
                for tp in range(16):
                    S.wait(t, "t", fin_t[tp])
                    S.wait(t, "t", tbank_at[tp])
                    inst = t.transpose(pb45[tp % 2][0:NE, 0:128],
                                       probs_sb[:, tp, :], idf_sb[:, :])
                    S.attach(inst, "sT")

            TB_(gate_t)

            def gate_c(c, gmm=tuple(gmm), nmx_t=tuple(nmx_t)):
                for tp in range(16):
                    a = 8 + 3 * (tp % 2)
                    S.wait(c, "c", nmx_t[tp])
                    i = c.activation(out=pexp_sb[:, tp % 2, :],
                                     in_=psE[tp % 4][:, 0:NE],
                                     func=AF.Exp, bias=stat_sb[:, a:a + 1],
                                     accum_out=stat_sb[:, a + 1:a + 2])
                    S.attach(i, "sS")

            CB_(gate_c)

            def gate_v(v, b=b, gmm=tuple(gmm), gex=tuple(gex), ptr_t=tuple(ptr_t),
                       fin_t=tuple(fin_t)):
                def emit_nmx(tp):
                    a = 8 + 3 * (tp % 2)
                    S.wait(v, "v", gmm[tp])
                    if tp >= 2:
                        S.wait(v, "v", gex[tp - 2])
                    i = v.reduce_max(stat_sb[:, a:a + 1], psE[tp % 4][:, 0:NE],
                                     axis=AX.X, negate=True)
                    S.attach(i, "sV")

                def emit_fin(tp):
                    a = 8 + 3 * (tp % 2)
                    S.wait(v, "v", gex[tp])
                    i = v.reciprocal(stat_sb[:, a + 2:a + 3],
                                     stat_sb[:, a + 1:a + 2])
                    S.attach(i, "sV")
                    v.wait_ge(S.h["sV"], fin_t[tp][1] - 1)
                    i = v.tensor_scalar(out=probs_sb[:, tp, :],
                                        in0=pexp_sb[:, tp % 2, :],
                                        scalar1=stat_sb[:, a + 2:a + 3],
                                        scalar2=None, op0=ALU.mult)
                    S.attach(i, "sV")

                emit_nmx(0)
                for tp in range(1, 16):
                    emit_nmx(tp)
                    emit_fin(tp - 1)
                emit_fin(15)
                for tp in range(16):
                    S.wait(v, "v", ptr_t[tp])
                    inst = v.tensor_copy(
                        P_all[b * 32:b * 32 + 8, tp * 128:(tp + 1) * 128],
                        pb45[tp % 2][0:NE, 0:128])
                    S.attach(inst, "sV")

            VB_(gate_v)
            gate_done_tok[b] = ptr_t[15]
            tok_Pall[b] = pcp_t[15]

            def meta_store(y, b=b):
                S.wait(y, "y", tok_Pall[b])
                S.wait(y, "y", tok_tid)
                dma(y, "y", "dY",
                    xgd_d[b, :, 1024:1040].bitcast(F32).rearrange(
                        "(t p) e -> p t e", p=128),
                    probs_sb[:, :, :])
                dma(y, "y", "dY",
                    xgd_d[b, :, 1040:1042].bitcast(U32).rearrange(
                        "(t p) one -> p t one", p=128),
                    tid_sb[:, :].unsqueeze(2),
                    allow_slow_non_contiguous=True)

            Y_(meta_store)
            tok_xgd_meta[b] = S.cur("dY")

        for cm in [x1r_cm, xbf_cm, x0r_cm, oT_cm, E_cm, v_cm, kvl_cm, kT_cm,
                   qT_cm, xT_cm, wuni_cm]:
            cm.__exit__(None, None, None)

        posu_sb, _ = sb("posu_sb", [128, 16, 64], U32)
        xgt_sb, _ = sb("xgt_sb", [128, 4, XGW], U16)
        xnat_sb, _ = sb("xnat_sb", [128, 2, 1024], BF16)
        w2_sb, _ = sb("w2_sb", [128, 32, D], BF16)
        w1r_sb, w1r_cm = sb("w1r_sb", [128, 2, 8, 1024], BF16)
        hTg_sb, hTg_cm = sb("hTg_sb", [128, 32, 512], BF16)
        xinT_sb, xinT_cm = sb("xinT_sb", [128, 2, 8, 512], BF16)
        meta_sb, meta_cm = sb("meta_sb", [128, 2, 4, 24], U16)
        y_sb, y_cm = sb("y_sb", [128, 1, D], F32)
        W1s, W1_cm = sb("W1s", [64, T], F32)
        W3s, W3_cm = sb("W3s", [64, T], F32)
        W4s = P_all
        W4_cm = None

        ffn_barrier = [S.cur("sT"), S.cur("sV"), S.cur("sS"), S.cur("dY"),
                       S.cur("dY2"), S.cur("dGx")]

        def bisect(v):
            for tk in ffn_barrier:
                S.wait(v, "v", tk)
            S.wait(v, "v", tok_Pall[0])
            S.wait(v, "v", tok_Pall[1])
            S.inc(v.memset(bis_sb[:, 0:1], 0.0), "sV")
            S.inc(v.memset(bis_sb[:, 1:2], 1.0), "sV")
            S.wait(v, "v", S.cur("sV"))
            for it in range(NBIS):
                i = v.tensor_tensor(out=bis_sb[:, 2:3], in0=bis_sb[:, 0:1],
                                    in1=bis_sb[:, 1:2], op=ALU.add)
                S.inc(i, "sV", 1)
                S.wait(v, "v", S.cur("sV"))
                i = v.tensor_scalar(out=bis_sb[:, 2:3], in0=bis_sb[:, 2:3],
                                    scalar1=0.5, scalar2=None, op0=ALU.mult)
                S.inc(i, "sV", 1)
                S.wait(v, "v", S.cur("sV"))
                i = v.tensor_scalar(out=W1s[:, :], in0=P_all[:, :],
                                    scalar1=bis_sb[0:64, 2:3], scalar2=0.0,
                                    op0=ALU.is_gt, op1=ALU.add,
                                    accum_out=bis_sb[0:64, 3:4])
                S.inc(i, "sV", 1)
                S.wait(v, "v", S.cur("sV"))
                i = v.tensor_scalar(out=bis_sb[:, 4:5], in0=bis_sb[:, 3:4],
                                    scalar1=float(KS), scalar2=None, op0=ALU.is_ge)
                S.inc(i, "sV", 1)
                S.wait(v, "v", S.cur("sV"))
                i = v.tensor_tensor(out=bis_sb[:, 6:7], in0=bis_sb[:, 2:3],
                                    in1=bis_sb[:, 0:1], op=ALU.subtract)
                S.inc(i, "sV", 1)
                S.wait(v, "v", S.cur("sV"))
                i = v.tensor_tensor(out=bis_sb[:, 6:7], in0=bis_sb[:, 6:7],
                                    in1=bis_sb[:, 4:5], op=ALU.mult)
                S.inc(i, "sV", 1)
                S.wait(v, "v", S.cur("sV"))
                i = v.tensor_tensor(out=bis_sb[:, 0:1], in0=bis_sb[:, 0:1],
                                    in1=bis_sb[:, 6:7], op=ALU.add)
                S.inc(i, "sV", 1)
                S.wait(v, "v", S.cur("sV"))
                i = v.tensor_scalar(out=bis_sb[:, 4:5], in0=bis_sb[:, 4:5],
                                    scalar1=-1.0, scalar2=1.0,
                                    op0=ALU.mult, op1=ALU.add)
                S.inc(i, "sV", 1)
                S.wait(v, "v", S.cur("sV"))
                i = v.tensor_tensor(out=bis_sb[:, 6:7], in0=bis_sb[:, 2:3],
                                    in1=bis_sb[:, 1:2], op=ALU.subtract)
                S.inc(i, "sV", 1)
                S.wait(v, "v", S.cur("sV"))
                i = v.tensor_tensor(out=bis_sb[:, 6:7], in0=bis_sb[:, 6:7],
                                    in1=bis_sb[:, 4:5], op=ALU.mult)
                S.inc(i, "sV", 1)
                S.wait(v, "v", S.cur("sV"))
                i = v.tensor_tensor(out=bis_sb[:, 1:2], in0=bis_sb[:, 1:2],
                                    in1=bis_sb[:, 6:7], op=ALU.add)
                S.inc(i, "sV", 1)
                S.wait(v, "v", S.cur("sV"))
            i = v.tensor_scalar(out=W1s[:, :], in0=P_all[:, :],
                                scalar1=bis_sb[0:64, 1:2], scalar2=0.0,
                                op0=ALU.is_gt, op1=ALU.add,
                                accum_out=bis_sb[0:64, 3:4])
            S.inc(i, "sV", 1)
            S.wait(v, "v", S.cur("sV"))
            i = v.tensor_scalar(out=bis_sb[:, 5:6], in0=bis_sb[:, 3:4],
                                scalar1=float(KS), scalar2=-1.0,
                                op0=ALU.subtract, op1=ALU.mult)
            S.inc(i, "sV", 1)
            i = v.tensor_scalar(out=W3s[:, :], in0=P_all[:, :],
                                scalar1=bis_sb[0:64, 1:2], scalar2=None,
                                op0=ALU.is_equal)
            S.inc(i, "sV", 1)
            S.wait(v, "v", S.cur("sV"))
            i = v.tensor_tensor_scan(W4s[:, :], W3s[:, :], W3s[:, :], 0.0,
                                     ALU.add, ALU.bypass)
            S.inc(i, "sV", 1)
            S.wait(v, "v", S.cur("sV"))
            i = v.tensor_scalar(out=W4s[:, :], in0=W4s[:, :],
                                scalar1=bis_sb[0:64, 5:6], scalar2=None,
                                op0=ALU.is_le)
            S.inc(i, "sV", 1)
            S.wait(v, "v", S.cur("sV"))
            i = v.tensor_tensor(out=W3s[:, :], in0=W3s[:, :], in1=W4s[:, :],
                                op=ALU.mult)
            S.inc(i, "sV", 1)
            S.wait(v, "v", S.cur("sV"))
            i = v.tensor_tensor(out=W1s[:, :], in0=W1s[:, :], in1=W3s[:, :],
                                op=ALU.add)
            S.inc(i, "sV", 1)
            S.wait(v, "v", S.cur("sV"))
            i = v.tensor_tensor_scan(W4s[:, :], W1s[:, :], W1s[:, :], 0.0,
                                     ALU.add, ALU.bypass)
            S.inc(i, "sV", 1)
            S.wait(v, "v", S.cur("sV"))
            i = v.tensor_tensor(out=W4s[:, :], in0=W4s[:, :], in1=W1s[:, :],
                                op=ALU.subtract)
            S.inc(i, "sV", 1)
            S.wait(v, "v", S.cur("sV"))
            i = v.tensor_scalar(out=W3s[:, :], in0=W1s[:, :], scalar1=-OOBF,
                                scalar2=OOBF, op0=ALU.mult, op1=ALU.add)
            S.inc(i, "sV", 1)
            S.wait(v, "v", S.cur("sV"))
            i = v.tensor_tensor(out=W4s[:, :], in0=W4s[:, :], in1=W3s[:, :],
                                op=ALU.add)
            S.inc(i, "sV", 1)

        V_(bisect)
        posx_tok = S.cur("sV")

        for tp in range(16):
            def pos_tr(t, tp=tp, posx_tok=posx_tok):
                S.wait(t, "t", posx_tok)
                S.wait(t, "t", pv_free[tp % 2])
                inst = t.transpose(pb45[tp % 2][:, 0:64],
                                   W4s[0:64, tp * 128:(tp + 1) * 128],
                                   idf_sb[0:64, 0:64])
                S.inc(inst, "sT", 1)

            T_(pos_tr)
            trt = S.cur("sT")

            def pos_cp(v, tp=tp, trt=trt):
                S.wait(v, "v", trt)
                inst = v.tensor_copy(posu_sb[:, tp, :], pb45[tp % 2][:, 0:64])
                S.inc(inst, "sV", 1)

            V_(pos_cp)
            pv_free[tp % 2] = S.cur("sV")
        posu_tok = S.cur("sV")
        pos_done_tok = S.cur("sT")
        for cm in [W3_cm, W1_cm]:
            cm.__exit__(None, None, None)
        l2a_sb, _ = sb("l2a_sb", [128, 2, D], F32)
        l2b_sb, _ = sb("l2b_sb", [128, 2, D], F32)

        def rezero(v):
            S.inc(v.memset(zsq_sb[:, :], 0.0), "sV")

        V_(rezero)
        rz_tok = S.cur("sV")

        def ffna_fill(z):
            S.wait(z, "z", rz_tok)
            for b in range(NB):
                for tp in range(16):
                    dma(z, "z", "dZ", ffna_d[b][tp * 128:(tp + 1) * 128, :],
                        zsq_sb[:, :])

        C_(ffna_fill)
        ffna_fill_tok = S.cur("dZ")

        NIT = NE * NB
        xgt_free = [None, None, None, None]
        xnat_free = [None, None]
        xin_scat_tok = [None] * NIT
        xinT_tok = [None] * NIT
        meta_tok = [None] * NIT
        xinT_used = [None] * NIT
        meta_used = [None] * NIT
        xnat_ldt = [None] * NIT
        trload_done = [None] * 4
        hT_done = [None] * NIT
        w1p_tok = {}
        w1p_used = {}
        w2_tok = [None] * (NE + 1)
        w2_used = [None] * NE
        ysb_free = [None]
        ln2_ok = False

        def emit_xin_scatter(it):
            e, bb = it // NB, it % NB
            slot = it % 4
            for tp in range(16):
                gpar = tp % 4

                xch = f"dYx{tp % 4}"

                def fl(y, bb=bb, tp=tp, gpar=gpar, xch=xch):
                    S.wait(y, "y", tok_xgd_x1[bb])
                    S.wait(y, "y", tok_xgd_meta[bb])
                    S.wait(y, "y", xgt_free[gpar])
                    dma(y, "y", xch, xgt_sb[:, gpar, :],
                        xgd_d[bb, tp * 128:(tp + 1) * 128, :])

                Y_(fl)
                ldtok = S.cur(xch)

                ch = f"dGs{gpar}"

                def fs(g, e=e, bb=bb, slot=slot, it=it, tp=tp, gpar=gpar,
                       ldtok=ldtok, ch=ch):
                    S.wait(g, "g", posu_tok)
                    S.wait(g, "g", ldtok)
                    if it >= 4 and tp < 4:
                        S.wait(g, "g", trload_done[slot])
                        S.wait(g, "g", meta_tok[it - 4])
                    S.wait(g, "g", S.cur(ch))
                    inst = g.indirect_dma_start(
                        out=xin_d[slot][:, :],
                        out_offset=IndirectOffsetOnAxis(
                            posu_sb[:, tp, bb * 32 + e:bb * 32 + e + 1], 0),
                        in_=xgt_sb[:, gpar, :],
                        in_offset=None,
                        bounds_check=bc_reg[0],
                        oob_is_err=False)
                    S.inc(inst, ch, 16)

                G_(fs)
                xgt_free[gpar] = S.cur(ch)
            xin_scat_tok[it] = [S.cur(f"dGs{i}") for i in range(4)]

        def emit_xnat_loads(it):
            slot = it % 4
            ldt = []
            for sp in range(2):
                npar = (it * 4 + sp) % 2
                nfree = xnat_free[npar]

                def fl(y, slot=slot, it=it, sp=sp, npar=npar, nfree=nfree):
                    for tk in xin_scat_tok[it]:
                        S.wait(y, "y", tk)
                    S.wait(y, "y", nfree)
                    dma(y, "y", "dY", xnat_sb[:, npar, :],
                        xin_d[slot][sp * 128:(sp + 1) * 128, 0:1024].bitcast(BF16))

                Y_(fl)
                ldt.append(S.cur("dY"))
            xnat_ldt[it] = ldt

        def emit_xin_tr(it):
            xpar = it % 2
            slot = it % 4
            trx = [S.reserve("sT") for _ in range(4)]
            evx = [S.reserve("sV") for _ in range(4)]
            ldt = list(xnat_ldt[it])
            for sp in range(2, 4):
                npar = (it * 4 + sp) % 2

                def fl(y, slot=slot, it=it, sp=sp, npar=npar, w=trx[sp - 2]):
                    S.wait(y, "y", w)
                    dma(y, "y", "dY", xnat_sb[:, npar, :],
                        xin_d[slot][sp * 128:(sp + 1) * 128, 0:1024].bitcast(BF16))

                Y_(fl)
                ldt.append(S.cur("dY"))
            trload_done[slot] = ldt[-1]
            psT_at = []
            for j in range(4):
                psT_at.append(psT_free[j % 2])
                psT_free[j % 2] = evx[j]
            xnat_free[0] = trx[2]
            xnat_free[1] = trx[3]

            def ftr(t, it=it, ldt=tuple(ldt), trx=tuple(trx),
                    psT_at=tuple(psT_at)):
                for sp in range(4):
                    npar = (it * 4 + sp) % 2
                    S.wait(t, "t", ldt[sp])
                    S.wait(t, "t", psT_at[sp])
                    for c in range(8):
                        inst = t.transpose(
                            psT[sp % 2][:, c * 128:(c + 1) * 128],
                            xnat_sb[:, npar, c * 128:(c + 1) * 128],
                            idb_sb[:, :])
                    S.attach(inst, "sT")

            TB_(ftr)

            def fev(v, it=it, xpar=xpar, trx=tuple(trx)):
                if it >= 2:
                    S.wait(v, "v", xinT_used[it - 2])
                for sp in range(4):
                    S.wait(v, "v", trx[sp])
                    inst = v.tensor_copy(
                        xinT_sb[:, xpar, :, sp * 128:(sp + 1) * 128],
                        psT[sp % 2][:, :].rearrange("p (a q) -> p a q", a=8))
                    S.attach(inst, "sV")

            VB_(fev)
            xinT_tok[it] = evx[3]

        def emit_xin_meta(it):
            slot = it % 4
            xpar = it % 2

            def fm(y, slot=slot, xpar=xpar, it=it):
                for tk in xin_scat_tok[it]:
                    S.wait(y, "y", tk)
                if it >= 2:
                    S.wait(y, "y", meta_used[it - 2])
                dma(y, "y", "dZ", meta_sb[:, xpar, :, :],
                    xin_d[slot][:, 1024:1048].rearrange("(j p) m -> p j m", p=128))

            Y_(fm)
            meta_tok[it] = S.cur("dZ")

        def emit_w1_piece(gpi):
            it, pf = gpi // 4, gpi % 4
            e = it // NB
            slot = gpi % 2

            wch = f"dYw{gpi % 2}"

            def f(y, e=e, pf=pf, slot=slot, gpi=gpi, wch=wch):
                for tk in ffn_barrier:
                    S.wait(y, "y", tk)
                S.wait(y, "y", wcast_tok[2 * e])
                if gpi - 2 in w1p_used and w1p_used[gpi - 2] is not None:
                    y.wait_ge(S.h["sT"], w1p_used[gpi - 2][1])
                dma(y, "y", wch, w1r_sb[:, slot, :, :],
                    w1bf_d[e, :, pf * 1024:(pf + 1) * 1024].rearrange(
                        "(c p) f -> p c f", p=128))

            Y_(f)
            w1p_tok[gpi] = S.cur(wch)

        def emit_w2_load(e):
            def f(y, e=e):
                for tk in ffn_barrier:
                    S.wait(y, "y", tk)
                S.wait(y, "y", wcast_tok[2 * e + 1])
                if e > 0:
                    y.wait_ge(S.h["sT"], w2_used[e - 1][1])
                for piece in range(4):
                    dma(y, "y", f"dYv{piece % 2}",
                        w2_sb[:, piece * 8:(piece + 1) * 8, :],
                        w2bf_d[e, piece * 1024:(piece + 1) * 1024, :].rearrange(
                            "(c p) d -> p c d", p=128))

            Y_(f)
            w2_tok[e] = [S.cur("dYv0"), S.cur("dYv1")]

        emit_w2_load(0)
        for gpi in range(2):
            emit_w1_piece(gpi)
        emit_xin_scatter(0)
        emit_xin_meta(0)
        emit_xin_scatter(1)
        emit_xin_meta(1)
        emit_xnat_loads(0)
        emit_xin_tr(0)

        prev_scat = None
        last_scat_b = [None, None]
        next_w1 = 2
        for it in range(NIT):
            e, bb = it // NB, it % NB
            xpar = it % 2
            hmm = [S.reserve("sT") for _ in range(32)]
            hge = [S.reserve("sS") for _ in range(32)]
            hent = list(bank_free) + [hge[j - 4] for j in range(4, 32)]
            bank_free = [hge[28], hge[29], hge[30], hge[31]]
            for pf in range(4):
                w1p_used[it * 4 + pf] = hmm[pf * 8 + 7]
            while next_w1 <= it * 4 + 3:
                emit_w1_piece(next_w1)
                next_w1 += 1

            def h_sec(t, it=it, xpar=xpar, hent=tuple(hent)):
                S.wait(t, "t", xinT_tok[it])
                for j in range(32):
                    pf, fc = j // 8, j % 8
                    gpi = it * 4 + pf
                    S.wait(t, "t", w1p_tok[gpi])
                    S.wait(t, "t", hent[j])
                    for k in range(8):
                        inst = t.matmul(
                            psE[j % 4][:, :],
                            w1r_sb[:, gpi % 2, k, fc * 128:(fc + 1) * 128],
                            xinT_sb[:, xpar, k, :],
                            start=(k == 0), stop=(k == 7))
                    S.attach(inst, "sT")

            TB_(h_sec)

            def g_sec(c, hmm=tuple(hmm)):
                for j in range(32):
                    S.wait(c, "c", hmm[j])
                    inst = c.activation(out=hTg_sb[:, j, :],
                                        in_=psE[j % 4][:, :], func=AF.Gelu)
                    S.attach(inst, "sS")

            CB_(g_sec)
            while next_w1 <= min(it * 4 + 5, NIT * 4 - 1):
                emit_w1_piece(next_w1)
                next_w1 += 1
            xinT_used[it] = hmm[31]
            hT_done[it] = hge[31]
            scat_toks = [S.reserve("dGf", 16) for _ in range(4)]
            meta_used[it] = scat_toks[3]
            if it + 1 < NIT:
                emit_xnat_loads(it + 1)
            if it + 2 < NIT:
                emit_xin_scatter(it + 2)
                emit_xin_meta(it + 2)

            ymm = [S.reserve("sT") for _ in range(8)]
            ysc = [S.reserve("sV", 2) for _ in range(4)]
            yent = [pv_free[0], pv_free[1]] + [ysc[(j - 2) // 2] for j in range(2, 8)]
            pv_free[0], pv_free[1] = ysc[3], ysc[3]

            def y_sec(t, it=it, e=e, yent=tuple(yent)):
                S.wait(t, "t", hT_done[it])
                for tk in w2_tok[e]:
                    S.wait(t, "t", tk)
                for j in range(8):
                    sp, dc = j // 2, j % 2
                    S.wait(t, "t", yent[j])
                    for fc in range(32):
                        inst = t.matmul(pb45[j % 2][:, :],
                                        hTg_sb[:, fc, sp * 128:(sp + 1) * 128],
                                        w2_sb[:, fc, dc * 512:(dc + 1) * 512],
                                        start=(fc == 0), stop=(fc == 31))
                    S.attach(inst, "sT")

            TB_(y_sec)

            def yv_sec(v, it=it, e=e, xpar=xpar, ymm=tuple(ymm), ysc=tuple(ysc),
                       yfree=ysb_free[0], scat_toks=tuple(scat_toks)):
                S.wait(v, "v", meta_tok[it])
                for sp in range(4):
                    g_ap = meta_sb[:, xpar, sp, 2 * e:2 * e + 2].bitcast(F32)
                    if sp == 0:
                        S.wait(v, "v", yfree)
                    else:
                        S.wait(v, "v", scat_toks[sp - 1])
                    for dc in range(2):
                        S.wait(v, "v", ymm[sp * 2 + dc])
                        inst = v.tensor_scalar(
                            out=y_sb[:, 0, dc * 512:(dc + 1) * 512],
                            in0=pb45[dc][:, :],
                            scalar1=g_ap, scalar2=None, op0=ALU.mult)
                        S.attach(inst, "sV")

            VB_(yv_sec)

            def y_scat_sec(g, it=it, e=e, bb=bb, xpar=xpar, ysc=tuple(ysc),
                           pscat=prev_scat):
                S.wait(g, "g", ffna_fill_tok)
                if pscat is not None:
                    S.wait(g, "g", pscat)
                for sp in range(4):
                    S.wait(g, "g", ysc[sp])
                    inst = g.indirect_dma_start(
                        out=ffna_d[bb][:, :],
                        out_offset=IndirectOffsetOnAxis(
                            meta_sb[:, xpar, sp, 16:18].bitcast(U32), 0),
                        in_=y_sb[:, 0, :],
                        in_offset=None,
                        compute_op=ALU.add)
                    S.attach(inst, "dGf", 16)
                    if sp < 3:
                        g.wait_ge(S.h["dGf"], scat_toks[sp][1])

            G_(y_scat_sec)
            prev_scat = scat_toks[3]
            ysb_free[0] = scat_toks[3]
            last_scat_b[bb] = scat_toks[3]
            w2_used[e] = ymm[7]
            if it + 1 < NIT:
                emit_xin_tr(it + 1)
            if bb == NB - 1 and e + 1 < NE:
                emit_w2_load(e + 1)
        all_scat = prev_scat

        l2_free = [None, None]
        for bb in range(NB):
            for tp in range(16):
                rpar = tp % 2

                lch = f"dY2{rpar}"

                def l2_load(y, bb=bb, tp=tp, rpar=rpar, lch=lch):
                    S.wait(y, "y", last_scat_b[bb])
                    S.wait(y, "y", pos_done_tok)
                    S.wait(y, "y", l2_free[rpar])
                    dma(y, "y", lch, l2a_sb[:, rpar, :],
                        x1sp_d[bb, tp * 128:(tp + 1) * 128, :])
                    dma(y, "y", lch, l2b_sb[:, rpar, :],
                        ffna_d[bb][tp * 128:(tp + 1) * 128, :])

                Y_(l2_load)
                ldtok = S.cur(lch)

                base = 1 + 7 * (tp % 2)

                def l2_add(v, rpar=rpar, ldtok=ldtok, base=base):
                    S.wait(v, "v", ldtok)
                    tile = l2a_sb[:, rpar, :]
                    i = v.scalar_tensor_tensor(out=tile, in0=l2b_sb[:, rpar, :],
                                               scalar=1.0 / D, in1=tile,
                                               op0=ALU.bypass, op1=ALU.add,
                                               accum_out=stat_sb[:, base:base + 1])
                    S.inc(i, "sV", 1)
                    S.wait(v, "v", S.cur("sV"))
                    i = v.scalar_tensor_tensor(out=zsq_sb[:, :], in0=tile,
                                               scalar=1.0 / D, in1=tile,
                                               op0=ALU.mult, op1=ALU.mult,
                                               accum_out=stat_sb[:, base + 1:base + 2])
                    S.inc(i, "sV", 1)
                    S.wait(v, "v", S.cur("sV"))
                    i = v.tensor_scalar(out=stat_sb[:, base:base + 1],
                                        in0=stat_sb[:, base:base + 1],
                                        scalar1=1.0 / D, scalar2=None, op0=ALU.mult)
                    S.inc(i, "sV", 1)
                    S.wait(v, "v", S.cur("sV"))
                    i = v.tensor_tensor(out=stat_sb[:, base + 2:base + 3],
                                        in0=stat_sb[:, base:base + 1],
                                        in1=stat_sb[:, base:base + 1], op=ALU.mult)
                    S.inc(i, "sV", 1)
                    S.wait(v, "v", S.cur("sV"))
                    i = v.tensor_tensor(out=stat_sb[:, base + 3:base + 4],
                                        in0=stat_sb[:, base + 1:base + 2],
                                        in1=stat_sb[:, base + 2:base + 3],
                                        op=ALU.subtract)
                    S.inc(i, "sV", 1)
                    S.wait(v, "v", S.cur("sV"))
                    emit_var_eps(v, base)

                V_(l2_add)
                vtok = S.cur("sV")
                stok = emit_rsqrt_sec(base, vtok)

                def l2_ap(v, rpar=rpar, base=base, stok=stok):
                    S.wait(v, "v", stok)
                    tile = l2a_sb[:, rpar, :]
                    i = v.tensor_scalar(out=tile, in0=tile,
                                        scalar1=stat_sb[:, base:base + 1],
                                        scalar2=stat_sb[:, base + 5:base + 6],
                                        op0=ALU.subtract, op1=ALU.mult)
                    S.inc(i, "sV", 1)

                V_(l2_ap)
                atok = S.cur("sV")

                def o_store(y, bb=bb, tp=tp, rpar=rpar, atok=atok):
                    S.wait(y, "y", atok)
                    dma(y, "y", "dO", out_d[bb, tp * 128:(tp + 1) * 128, :],
                        l2a_sb[:, rpar, :])

                Y_(o_store)
                l2_free[rpar] = S.cur("dO")

        def fin(y):
            S.wait(y, "y", S.cur("dO"))

        Y_(fin)

    return nc


def kernel(L0_attn_x, L0_attn_Wq, L0_attn_Wkv_down, L0_attn_Wk_up, L0_attn_Wv_up,
           L0_attn_Wo, L0_ffn_gate_w, L0_ffn_w1, L0_ffn_w2):
    x = np.ascontiguousarray(L0_attn_x)
    wqT = np.ascontiguousarray(np.asarray(L0_attn_Wq).T)
    wkvT = np.ascontiguousarray(np.asarray(L0_attn_Wkv_down).T)
    wkT = np.ascontiguousarray(np.asarray(L0_attn_Wk_up).T)
    wvT = np.ascontiguousarray(np.asarray(L0_attn_Wv_up).T)
    woT = np.ascontiguousarray(np.asarray(L0_attn_Wo).T)
    gwT = np.ascontiguousarray(np.asarray(L0_ffn_gate_w).T)
    w1 = np.ascontiguousarray(L0_ffn_w1)
    w2 = np.ascontiguousarray(L0_ffn_w2)
    ident = np.eye(128, dtype=np.float32)

    nc = build()
    in_maps = []
    for c in range(8):
        in_maps.append({
            "x": np.ascontiguousarray(x[2 * c:2 * c + 2]),
            "wqT": wqT, "wkvT": wkvT, "wkT": wkT, "wvT": wvT, "woT": woT,
            "gwT": gwT, "w1": w1, "w2": w2, "ident": ident,
        })
    res = run_bass_kernel_spmd(nc, in_maps, core_ids=list(range(8)))
    out = np.concatenate([r["out"] for r in res.results], axis=0)
    return out.reshape(16, T, D).astype(np.float32)


# version 17
# speedup vs baseline: 1.1141x; 1.1141x over previous
import numpy as np
import concourse.bass as bass
import concourse.mybir as mybir
from concourse.bass_utils import run_bass_kernel_spmd
from concourse.bass import IndirectOffsetOnAxis

F32 = mybir.dt.float32
BF16 = mybir.dt.bfloat16
U32 = mybir.dt.uint32
U16 = mybir.dt.uint16
AF = mybir.ActivationFunctionType
ALU = mybir.AluOpType
AX = mybir.AxisListType

NB = 2
T = 2048
D = 1024
H = 8
HD = 256
LR = 64
FF = 4096
NE = 8
KS = 512
XGW = 1048
NBIS = 38

OOBF = 1.0e9


class Sems:
    def __init__(self, nc, stack):
        self.h = {}
        self.n = {}
        self.lastw = {}
        self.nc = nc
        self.stack = stack

    def new(self, name):
        self.h[name] = self.stack.enter_context(self.nc.semaphore(name))
        self.n[name] = 0

    def inc(self, inst, name, k=1):
        inst.then_inc(self.h[name], k)
        self.n[name] += k
        return (name, self.n[name])

    def cur(self, name):
        return (name, self.n[name])

    def reserve(self, name, k=1):
        self.n[name] += k
        return (name, self.n[name])

    def attach(self, inst, name, k=1):
        inst.then_inc(self.h[name], k)

    def wait(self, eng, ename, tok):
        if tok is None:
            return
        name, v = tok
        if v <= 0:
            return
        key = (ename, name)
        if self.lastw.get(key, -1) >= v:
            return
        eng.wait_ge(self.h[name], v)
        self.lastw[key] = v


def build(debug=False):
    nc = bass.Bass("TRN2", target_bir_lowering=False)

    x_d = nc.dram_tensor("x", [NB, T, D], F32, kind="ExternalInput").ap()
    wqT_d = nc.dram_tensor("wqT", [D, H * HD], F32, kind="ExternalInput").ap()
    wkvT_d = nc.dram_tensor("wkvT", [D, LR], F32, kind="ExternalInput").ap()
    wkT_d = nc.dram_tensor("wkT", [LR, H * HD], F32, kind="ExternalInput").ap()
    wvT_d = nc.dram_tensor("wvT", [LR, H * HD], F32, kind="ExternalInput").ap()
    woT_d = nc.dram_tensor("woT", [H * HD, D], F32, kind="ExternalInput").ap()
    gwT_d = nc.dram_tensor("gwT", [D, NE], F32, kind="ExternalInput").ap()
    w1_d = nc.dram_tensor("w1", [NE, D, FF], F32, kind="ExternalInput").ap()
    w2_d = nc.dram_tensor("w2", [NE, FF, D], F32, kind="ExternalInput").ap()
    id_d = nc.dram_tensor("ident", [128, 128], F32, kind="ExternalInput").ap()

    out_d = nc.dram_tensor("out", [NB, T, D], F32, kind="ExternalOutput").ap()

    
    xgd_d = nc.dram_tensor("xgd", [NB, T, XGW], U16, kind="Internal").ap()
    x1sp_d = nc.dram_tensor("x1sp", [NB, T, D], F32, kind="Internal").ap()
    ffna_d = [nc.dram_tensor(f"ffna{b}", [T, D], F32, kind="Internal").ap()
              for b in range(NB)]
    xin_d = [nc.dram_tensor(f"xin{s}", [KS, XGW], U16, kind="Internal").ap()
             for s in range(4)]

    from contextlib import ExitStack
    st = ExitStack()
    S = Sems(nc, st)
    for s in ["sT", "sV", "sS", "sG", "dY", "dY2", "dY20", "dY21", "dYx0", "dYx1", "dYx2", "dYx3", "dYw0", "dYw1", "dYw2", "dYw3", "dYv0", "dYv1", "dYv2", "dYv3", "dZ", "dGw", "dGx", "dGs0", "dGs1", "dGs2", "dGs3", "dGf", "dO"]:
        S.new(s)

    def sb(name, shape, dtype):
        cm = nc.sbuf_tensor(name, shape, dtype)
        t = cm.__enter__()
        return t, cm

    def psum(name, shape, dtype):
        cm = nc.psum_tensor(name, shape, dtype)
        t = cm.__enter__()
        return t, cm

    wkv_sb, _ = sb("wkv_sb", [128, 8, LR], BF16)
    wk_sb, _ = sb("wk_sb", [64, H * HD], BF16)
    wv_sb, _ = sb("wv_sb", [64, H * HD], BF16)
    gw_sb, _ = sb("gw_sb", [128, 8, NE], BF16)
    idb_sb, _ = sb("idb_sb", [128, 128], BF16)
    idf_sb, _ = sb("idf_sb", [128, 128], F32)
    zsq_sb, _ = sb("zsq_sb", [128, D], F32)
    stat_sb, _ = sb("stat_sb", [128, 16], F32)
    rc_sb, _ = sb("rc_sb", [128, 2], F32)
    obf_sb, _ = sb("obf_sb", [128, 2, HD], BF16)
    P_all, _ = sb("P_all", [64, T], F32)
    probs_sb, _ = sb("probs_sb", [128, 16, NE], F32)
    pexp_sb, _ = sb("pexp_sb", [128, 2, NE], F32)
    tid_sb, _ = sb("tid_sb", [128, 16], U32)
    bis_sb, _ = sb("bis_sb", [64, 8], F32)

    psE_t, _ = psum("psE", [128, 2048], F32)
    psE = [psE_t[:, q * 512:(q + 1) * 512] for q in range(4)]
    pb4_t, _ = psum("pb4", [128, 512], F32)
    pb5_t, _ = psum("pb5", [128, 512], F32)
    pb45 = [pb4_t, pb5_t]
    psT = []
    for i in range(2):
        t, _ = psum(f"psT{i}", [128, 1024], BF16)
        psT.append(t)

    bank_free = [None] * 4
    pv_free = [None] * 2
    psT_free = [None] * 2

    with nc.Block() as blk:
        def T_(fn):
            blk.tensor(fn)

        def V_(fn):
            pre = S.cur("sV")

            def w(v):
                S.wait(v, "v", pre)
                fn(v)

            blk.vector(w)

        def C_(fn):
            pre = S.cur("sS")

            def w(c):
                S.wait(c, "c", pre)
                fn(c)

            blk.scalar(w)

        def G_(fn):
            blk.gpsimd(fn)

        def Y_(fn):
            blk.sync(fn)

        def TB_(fn):
            blk.tensor(fn)

        def VB_(fn):
            blk.vector(fn)

        def CB_(fn):
            blk.scalar(fn)

        ENAME = {}

        def dma(eng, ename, sem, out, in_, **kw):
            S.wait(eng, ename, S.cur(sem))
            inst = eng.dma_start(out=out, in_=in_, **kw)
            return S.inc(inst, sem, 16)

        def g_dma(g, out, in_, sem, **kw):
            return dma(g, "g", sem, out, in_, **kw)

        bc_reg = [None]

        def setup(g):
            bc_reg[0] = g.alloc_register("bc")
            g.reg_mov(bc_reg[0], KS - 1)
            g_dma(g, wkv_sb[:, :, :], wkvT_d.rearrange("(c p) l -> p c l", p=128), "dGw")
            g_dma(g, wk_sb[:, :], wkT_d[:, :], "dGw")
            g_dma(g, wv_sb[:, :], wvT_d[:, :], "dGw")
            g_dma(g, gw_sb[:, :, :], gwT_d.rearrange("(c p) e -> p c e", p=128), "dGw")
            g_dma(g, idb_sb[:, :], id_d[:, :], "dGw")
            g_dma(g, idf_sb[:, :], id_d[:, :], "dGw")

        G_(setup)
        tok_smallw = S.cur("dGw")

        def setup_g2(g):
            inst = g.iota(tid_sb[:, :], pattern=[[128, 16]], base=0,
                          channel_multiplier=1)
            S.inc(inst, "sG", 1)

        G_(setup_g2)
        tok_tid = S.cur("sG")

        def setup_v(v):
            S.inc(v.memset(zsq_sb[:, :], 0.0), "sV")
            S.inc(v.memset(stat_sb[:, 7:8], 1e-5), "sV")
            S.inc(v.memset(P_all[:, :], 0.0), "sV")

        V_(setup_v)
        tok_zero = S.cur("sV")

        def setup_z(z):
            S.wait(z, "z", tok_zero)
            zb = zsq_sb[:, 0:48].bitcast(U16).rearrange("p (t m) -> p t m", m=6)
            for b in range(NB):
                dma(z, "z", "dZ",
                    xgd_d[b, :, 1042:1048].rearrange("(t p) m -> p t m", p=128),
                    zb)

        C_(setup_z)
        tok_ffna_zero = S.cur("dZ")

        wuni_sb, wuni_cm = sb("wuni_sb", [128, 16384], BF16)
        wq_view = wuni_sb.ap().rearrange("p (c f) -> p c f", c=8)
        wo_view = wuni_sb.ap().rearrange("p (c f) -> p c f", c=16)
        xT_sb, xT_cm = sb("xT_sb", [128, 8, T], BF16)
        qT_sb, qT_cm = sb("qT_sb", [128, 2, T], BF16)
        kT_sb, kT_cm = sb("kT_sb", [128, 2, T], BF16)
        kvl_sb, kvl_cm = sb("kvl_sb", [64, T], BF16)
        v_sb, v_cm = sb("v_sb", [128, 16, HD + 1], BF16)
        E_sb, E_cm = sb("E_sb", [128, 1, 16, 512], BF16)
        oT_sb, oT_cm = sb("oT_sb", [128, 16, T], BF16)
        x0r_sb, x0r_cm = sb("x0r_sb", [128, 1, D], F32)
        xbf_sb, xbf_cm = sb("xbf_sb", [128, 2, D], BF16)
        x1r_sb, x1r_cm = sb("x1r_sb", [128, 1, D], F32)

        def setup_v2(v):
            S.inc(v.memset(v_sb[:, :, HD:HD + 1], 1.0), "sV")

        V_(setup_v2)
        tok_vones = S.cur("sV")

        def wq_load0(g):
            g_dma(g, wq_view, wqT_d.rearrange("(c p) f -> p c f", p=128), "dGw")

        G_(wq_load0)
        tok_wq = [S.cur("dGw"), None]
        tok_wo = [None, None]

        tok_xT = [None, None]
        tok_x1sp = [None, None]
        tok_xgd_x1 = [None, None]
        tok_xgd_meta = [None, None]
        tok_x1T = [None, None]
        tok_Pall = [None, None]

        E_free = [None, None]
        obf_free = [None, None]
        qkv_free = None
        wq_used = [None, None]
        gate_done_tok = [None, None]
        x0r_free = [None, None]
        xbf_war = [None, None]

        xT_psT_par = [0]

        def emit_xT_tile(src_f32_tile_emitter, dst_col, xbf_slot, extra_war=None):
            par = xT_psT_par[0]
            xT_psT_par[0] ^= 1
            ctok = S.reserve("sV")
            trq = S.reserve("sT")
            evq = S.reserve("sV")
            psT_at = psT_free[par]
            psT_free[par] = evq

            def vcast(v):
                src_f32_tile_emitter(v)

            def ftr(t, psT_at=psT_at, ctok=ctok, par=par, xbf_slot=xbf_slot):
                S.wait(t, "t", tok_smallw)
                S.wait(t, "t", ctok)
                S.wait(t, "t", psT_at)
                for c in range(8):
                    inst = t.transpose(psT[par][:, c * 128:(c + 1) * 128],
                                       xbf_sb[:, xbf_slot, c * 128:(c + 1) * 128],
                                       idb_sb[:, :])
                S.attach(inst, "sT")

            def fev(v, trq=trq, dst_col=dst_col, extra_war=extra_war, par=par):
                if extra_war is not None:
                    S.wait(v, "v", extra_war)
                S.wait(v, "v", trq)
                inst = v.tensor_copy(
                    xT_sb[:, :, dst_col:dst_col + 128],
                    psT[par][:, :].rearrange("p (a q) -> p a q", a=8))
                S.attach(inst, "sV")

            VB_(vcast)
            TB_(ftr)
            VB_(fev)
            return ctok, trq, evq


        def emit_var_eps(v, base):
            va = stat_sb[:, base + 3:base + 4]
            i = v.tensor_scalar(out=va, in0=va, scalar1=1e-5, scalar2=None,
                                op0=ALU.add)
            S.inc(i, "sV", 1)
            S.wait(v, "v", S.cur("sV"))
            i = v.reciprocal(stat_sb[:, base + 4:base + 5], va)
            S.inc(i, "sV", 1)

        def emit_rsqrt_sec(base, vtok):
            def rs(c, base=base, vtok=vtok):
                S.wait(c, "c", vtok)
                i = c.activation(out=stat_sb[:, base + 5:base + 6],
                                 in_=stat_sb[:, base + 4:base + 5],
                                 func=AF.Sqrt)
                S.inc(i, "sS", 1)

            C_(rs)
            return S.cur("sS")

        for b in range(NB):
            for tp in range(16):
                rp = tp % 2

                def x0_load(y, b=b, tp=tp):
                    S.wait(y, "y", x0r_free[0])
                    dma(y, "y", "dY2", x0r_sb[:, 0, :],
                        x_d[b, tp * 128:(tp + 1) * 128, :])

                Y_(x0_load)
                ldtok = S.cur("dY2")

                def cast_em(v, ldtok=ldtok, xw=xbf_war[rp], rp=rp):
                    S.wait(v, "v", ldtok)
                    if xw is not None:
                        S.wait(v, "v", xw)
                    inst = v.tensor_copy(xbf_sb[:, rp, :], x0r_sb[:, 0, :])
                    S.attach(inst, "sV")

                ctok, trq, evq = emit_xT_tile(
                    cast_em, tp * 128, rp,
                    gate_done_tok[0] if b == 1 else None)
                xbf_war[rp] = trq
                x0r_free[0] = ctok
            tok_xT[b] = evq

            if b == 1:
                def wq_load1(g):
                    g.wait_ge(S.h["sT"], gate_done_tok[0][1])
                    g_dma(g, wq_view,
                          wqT_d.rearrange("(c p) f -> p c f", p=128), "dGw")

                G_(wq_load1)
                tok_wq[1] = S.cur("dGw")


            for h in range(H):
                jobs = []
                if h == 0:
                    jobs += [("kvl", qc) for qc in range(4)]
                jobs += [("q", (m, qc)) for m in range(2) for qc in range(4)]
                jobs += [("k", (m, qc)) for m in range(2) for qc in range(4)]
                jobs += [("v", kp) for kp in range(16)]
                pmm = [S.reserve("sT") for _ in jobs]
                pev = [S.reserve("sV") for _ in jobs]
                ent_at = []
                ent = list(bank_free)
                for i in range(len(jobs)):
                    ent_at.append(ent[i % 4])
                    ent[i % 4] = pev[i]
                bank_free = ent
                nkvl = 4 if h == 0 else 0
                if h == 0:
                    kvl_tok = pev[3]
                v_tok = pev[-1]

                def proj_sec(t, b=b, h=h, jobs=tuple(jobs), ent_at=tuple(ent_at),
                             qf=qkv_free, kvt=kvl_tok):
                    S.wait(t, "t", tok_smallw)
                    S.wait(t, "t", tok_wq[b])
                    S.wait(t, "t", tok_xT[b])
                    if qf is not None:
                        S.wait(t, "t", qf)
                    for i, (kind, pr) in enumerate(jobs):
                        S.wait(t, "t", ent_at[i])
                        bk = i % 4
                        if kind == "kvl":
                            qc = pr
                            for k in range(8):
                                inst = t.matmul(psE[bk][0:64, :], wkv_sb[:, k, :],
                                                xT_sb[:, k, qc * 512:(qc + 1) * 512],
                                                start=(k == 0), stop=(k == 7))
                        elif kind == "q":
                            m, qc = pr
                            for k in range(8):
                                inst = t.matmul(
                                    psE[bk][:, :],
                                    wq_view[:, k, h * 256 + m * 128: h * 256 + (m + 1) * 128],
                                    xT_sb[:, k, qc * 512:(qc + 1) * 512],
                                    start=(k == 0), stop=(k == 7))
                        elif kind == "k":
                            m, qc = pr
                            S.wait(t, "t", kvt)
                            inst = t.matmul(
                                psE[bk][:, :],
                                wk_sb[0:64, h * 256 + m * 128: h * 256 + (m + 1) * 128],
                                kvl_sb[0:64, qc * 512:(qc + 1) * 512],
                                start=True, stop=True)
                        else:
                            kp = pr
                            S.wait(t, "t", kvt)
                            inst = t.matmul(psE[bk][:, 0:HD],
                                            kvl_sb[0:64, kp * 128:(kp + 1) * 128],
                                            wv_sb[0:64, h * 256:(h + 1) * 256],
                                            start=True, stop=True)
                        S.attach(inst, "sT")

                TB_(proj_sec)

                def proj_ev(v, h=h, jobs=tuple(jobs), pmm=tuple(pmm),
                            pev=tuple(pev)):
                    S.wait(v, "v", tok_vones)
                    for i, (kind, pr) in enumerate(jobs):
                        S.wait(v, "v", pmm[i])
                        bk = i % 4
                        if kind == "kvl":
                            qc = pr
                            inst = v.tensor_copy(
                                kvl_sb[0:64, qc * 512:(qc + 1) * 512],
                                psE[bk][0:64, :])
                        elif kind == "q":
                            m, qc = pr
                            inst = v.tensor_copy(
                                qT_sb[:, m, qc * 512:(qc + 1) * 512], psE[bk][:, :])
                        elif kind == "k":
                            m, qc = pr
                            inst = v.tensor_copy(
                                kT_sb[:, m, qc * 512:(qc + 1) * 512], psE[bk][:, :])
                        else:
                            kp = pr
                            inst = v.tensor_copy(v_sb[:, kp, 0:HD], psE[bk][:, 0:HD])
                        S.attach(inst, "sV")

                VB_(proj_ev)
                wq_used[b] = pmm[nkvl + 7]

                for qc in range(4):
                    smm = [S.reserve("sT") for _ in range(16)]
                    sexp = [S.reserve("sS") for _ in range(16)]
                    t_order = ["pv0", "pv1", "tr0", "pv2", "tr1", "pv3", "tr2", "tr3"]
                    ptk = {nm: S.reserve("sT") for nm in t_order}
                    v_order = ["nrm0", "nrm1", "ocp0", "nrm2", "ocp1", "nrm3",
                               "ocp2", "ocp3"]
                    vtk = {}
                    for nm in v_order:
                        vtk[nm] = S.reserve("sV", 2 if nm.startswith("nrm") else 1)
                    sent = list(bank_free)
                    sent_at = []
                    for kp in range(16):
                        sent_at.append(sent[kp % 4])
                        sent[kp % 4] = sexp[kp]
                    bank_free = sent
                    pv_ent = [pv_free[0], pv_free[1]]
                    pv_at = []
                    for qp in range(4):
                        pv_at.append(pv_ent[qp % 2])
                        pv_ent[qp % 2] = vtk[f"nrm{qp}"]
                    pv_free[0], pv_free[1] = pv_ent[0], pv_ent[1]
                    psT_at = [psT_free[0], psT_free[1]]
                    obf_at = [obf_free[0], obf_free[1]]
                    for qp in range(4):
                        pp = qp % 2
                        psT_at.append(psT_free[pp])
                        obf_at.append(obf_free[pp])
                        psT_free[pp] = vtk[f"ocp{qp}"]
                        obf_free[pp] = ptk[f"tr{qp}"]
                    psT_at = psT_at[2:]
                    obf_at = obf_at[2:]

                    def sc_sec(t, h=h, qc=qc, smm=tuple(smm), sexp=tuple(sexp),
                               sent_at=tuple(sent_at), pv_at=tuple(pv_at),
                               psT_at=tuple(psT_at), ptk=dict(ptk), vtk=dict(vtk),
                               efree=E_free[0], v_tok=v_tok):
                        S.wait(t, "t", efree)
                        for kp in range(16):
                            S.wait(t, "t", sent_at[kp])
                            bk = kp % 4
                            for m in range(2):
                                inst = t.matmul(psE[bk][:, :],
                                                kT_sb[:, m, kp * 128:(kp + 1) * 128],
                                                qT_sb[:, m, qc * 512:(qc + 1) * 512],
                                                start=(m == 0), stop=(m == 1))
                            S.attach(inst, "sT")

                        def emit_pv(qp):
                            bk = qp % 2
                            S.wait(t, "t", sexp[15])
                            S.wait(t, "t", v_tok)
                            S.wait(t, "t", pv_at[qp])
                            for kp in range(16):
                                inst = t.matmul(
                                    pb45[bk][:, 0:HD + 1],
                                    E_sb[:, 0, kp, qp * 128:(qp + 1) * 128],
                                    v_sb[:, kp, :],
                                    start=(kp == 0), stop=(kp == 15))
                            S.attach(inst, "sT")

                        def emit_tr(qp):
                            par = qp % 2
                            S.wait(t, "t", vtk[f"nrm{qp}"])
                            S.wait(t, "t", psT_at[qp])
                            t.transpose(psT[par][:, 0:128], obf_sb[:, par, 0:128],
                                        idb_sb[:, :])
                            inst = t.transpose(psT[par][:, 128:256],
                                               obf_sb[:, par, 128:256],
                                               idb_sb[:, :])
                            S.attach(inst, "sT")

                        emit_pv(0)
                        emit_pv(1)
                        emit_tr(0)
                        emit_pv(2)
                        emit_tr(1)
                        emit_pv(3)
                        emit_tr(2)
                        emit_tr(3)

                    TB_(sc_sec)

                    def exp_sec(c, qc=qc, smm=tuple(smm)):
                        for kp in range(16):
                            S.wait(c, "c", smm[kp])
                            inst = c.activation(out=E_sb[:, 0, kp, :],
                                                in_=psE[kp % 4][:, :],
                                                func=AF.Exp, scale=1.0 / 16.0)
                            S.attach(inst, "sS")

                    CB_(exp_sec)

                    def pvv_sec(v, h=h, qc=qc, ptk=dict(ptk), vtk=dict(vtk),
                                obf_at=tuple(obf_at)):
                        def emit_nrm(qp):
                            bk = qp % 2
                            par = qp % 2
                            S.wait(v, "v", ptk[f"pv{qp}"])
                            S.wait(v, "v", obf_at[qp])
                            i1 = v.reciprocal(rc_sb[:, par:par + 1],
                                              pb45[bk][:, HD:HD + 1])
                            S.attach(i1, "sV")
                            v.wait_ge(S.h["sV"], vtk[f"nrm{qp}"][1] - 1)
                            i2 = v.tensor_scalar(out=obf_sb[:, par, :],
                                                 in0=pb45[bk][:, 0:HD],
                                                 scalar1=rc_sb[:, par:par + 1],
                                                 scalar2=None, op0=ALU.mult)
                            S.attach(i2, "sV")

                        def emit_ocp(qp):
                            par = qp % 2
                            qglob = qc * 512 + qp * 128
                            S.wait(v, "v", ptk[f"tr{qp}"])
                            inst = v.tensor_copy(
                                oT_sb[:, 2 * h:2 * h + 2, qglob:qglob + 128],
                                psT[par][:, 0:256].rearrange("p (a q) -> p a q", a=2))
                            S.attach(inst, "sV")

                        emit_nrm(0)
                        emit_nrm(1)
                        emit_ocp(0)
                        emit_nrm(2)
                        emit_ocp(1)
                        emit_nrm(3)
                        emit_ocp(2)
                        emit_ocp(3)

                    VB_(pvv_sec)
                    E_free[0] = ptk["pv3"]
                qkv_free = ptk["tr3"]
            oT_ready = S.cur("sV")

            def wo_load(g, b=b):
                g.wait_ge(S.h["sT"], wq_used[b][1])
                g_dma(g, wo_view, woT_d.rearrange("(c p) f -> p c f", p=128), "dGw")

            G_(wo_load)
            tok_wo[b] = S.cur("dGw")

            x1r_free = [None]
            x1r_free2 = None
            for qp in range(16):
                rpar = 0

                def x0_pref(y, b=b, qp=qp):
                    S.wait(y, "y", x0r_free[0])
                    dma(y, "y", "dY2", x0r_sb[:, 0, :],
                        x_d[b, qp * 128:(qp + 1) * 128, :])

                Y_(x0_pref)
                x0tok = S.cur("dY2")

                mmtoks = []
                for dc in range(2):
                    bk = dc

                    def wo_mm(t, b=b, qp=qp, dc=dc, bk=bk, oT_ready=oT_ready):
                        S.wait(t, "t", oT_ready)
                        S.wait(t, "t", tok_wo[b])
                        S.wait(t, "t", pv_free[bk])
                        for kc in range(16):
                            inst = t.matmul(pb45[bk][:, :],
                                            oT_sb[:, kc, qp * 128:(qp + 1) * 128],
                                            wo_view[:, kc, dc * 512:(dc + 1) * 512],
                                            start=(kc == 0), stop=(kc == 15))
                        S.inc(inst, "sT", 1)

                    T_(wo_mm)
                    mmtoks.append(S.cur("sT"))

                def resid(v, qp=qp, rpar=rpar, x0tok=x0tok, mmtoks=tuple(mmtoks)):
                    S.wait(v, "v", x0tok)
                    S.wait(v, "v", x1r_free[rpar])
                    S.wait(v, "v", x1r_free2)
                    for dc in range(2):
                        S.wait(v, "v", mmtoks[dc])
                        inst = v.tensor_tensor(out=x1r_sb[:, rpar, dc * 512:(dc + 1) * 512],
                                               in0=pb45[dc][:, :],
                                               in1=x0r_sb[:, 0, dc * 512:(dc + 1) * 512],
                                               op=ALU.add)
                        S.inc(inst, "sV", 1)

                V_(resid)
                pv_free[0] = S.cur("sV")
                pv_free[1] = S.cur("sV")
                x0r_free[0] = S.cur("sV")
                rtok = S.cur("sV")

                base = 1 + 7 * (qp % 2)

                def ln1(v, rpar=rpar, rtok=rtok, base=base):
                    S.wait(v, "v", rtok)
                    S.wait(v, "v", tok_ffna_zero)
                    tile = x1r_sb[:, rpar, :]
                    i = v.tensor_scalar(out=zsq_sb[:, :], in0=tile, scalar1=1.0 / D,
                                        scalar2=0.0, op0=ALU.mult, op1=ALU.add,
                                        accum_out=stat_sb[:, base:base + 1])
                    S.inc(i, "sV", 1)
                    S.wait(v, "v", S.cur("sV"))
                    i = v.scalar_tensor_tensor(out=zsq_sb[:, :], in0=tile,
                                               scalar=1.0 / D, in1=tile,
                                               op0=ALU.mult, op1=ALU.mult,
                                               accum_out=stat_sb[:, base + 1:base + 2])
                    S.inc(i, "sV", 1)
                    S.wait(v, "v", S.cur("sV"))
                    i = v.tensor_tensor(out=stat_sb[:, base + 2:base + 3],
                                        in0=stat_sb[:, base:base + 1],
                                        in1=stat_sb[:, base:base + 1], op=ALU.mult)
                    S.inc(i, "sV", 1)
                    S.wait(v, "v", S.cur("sV"))
                    i = v.tensor_tensor(out=stat_sb[:, base + 3:base + 4],
                                        in0=stat_sb[:, base + 1:base + 2],
                                        in1=stat_sb[:, base + 2:base + 3],
                                        op=ALU.subtract)
                    S.inc(i, "sV", 1)
                    S.wait(v, "v", S.cur("sV"))
                    emit_var_eps(v, base)

                V_(ln1)
                vtok = S.cur("sV")
                stok = emit_rsqrt_sec(base, vtok)

                def ln1_ap(v, rpar=rpar, base=base, stok=stok):
                    S.wait(v, "v", stok)
                    tile = x1r_sb[:, rpar, :]
                    i = v.tensor_scalar(out=tile, in0=tile,
                                        scalar1=stat_sb[:, base:base + 1],
                                        scalar2=stat_sb[:, base + 5:base + 6],
                                        op0=ALU.subtract, op1=ALU.mult)
                    S.inc(i, "sV", 1)

                V_(ln1_ap)
                atok = S.cur("sV")

                def x1_store(y, b=b, qp=qp, rpar=rpar, atok=atok):
                    S.wait(y, "y", atok)
                    dma(y, "y", "dY", x1sp_d[b, qp * 128:(qp + 1) * 128, :],
                        x1r_sb[:, rpar, :])

                Y_(x1_store)
                store_tok = S.cur("dY")

                xrp = qp % 2

                def cast1_em(v, atok=atok, xw=xbf_war[xrp], xrp=xrp):
                    S.wait(v, "v", atok)
                    if xw is not None:
                        S.wait(v, "v", xw)
                    inst = v.tensor_copy(xbf_sb[:, xrp, :], x1r_sb[:, 0, :])
                    S.attach(inst, "sV")

                ctok, trq, evq = emit_xT_tile(cast1_em, qp * 128, xrp)
                xbf_war[xrp] = trq
                x1r_free[rpar] = store_tok
                x1r_free2 = ctok
            tok_x1sp[b] = S.cur("dY")
            tok_x1T[b] = evq



            def xg_cast(g, b=b):
                g.wait_ge(S.h["dY"], tok_x1sp[b][1])
                g_dma(g, xgd_d[b, :, 0:1024].bitcast(BF16), x1sp_d[b, :, :], "dGx")

            G_(xg_cast)
            tok_xgd_x1[b] = S.cur("dGx")

            gmm = [S.reserve("sT") for _ in range(16)]
            nmx_t = [None] * 16
            fin_t = [None] * 16
            nmx_t[0] = S.reserve("sV")
            for tp in range(1, 16):
                nmx_t[tp] = S.reserve("sV")
                fin_t[tp - 1] = S.reserve("sV", 2)
            fin_t[15] = S.reserve("sV", 2)
            gex = [S.reserve("sS") for _ in range(16)]
            ptr_t = [S.reserve("sT") for _ in range(16)]
            pcp_t = [S.reserve("sV") for _ in range(16)]
            gbank_at = list(bank_free) + [gex[tp - 4] for tp in range(4, 16)]
            tbank_at = [pv_free[0], pv_free[1]] + [pcp_t[tp - 2] for tp in range(2, 16)]
            bank_free = [gex[12], gex[13], gex[14], gex[15]]
            pv_free[0], pv_free[1] = pcp_t[14], pcp_t[15]

            def gate_t(t, b=b, gbank_at=tuple(gbank_at), tbank_at=tuple(tbank_at),
                       fin_t=tuple(fin_t)):
                S.wait(t, "t", tok_x1T[b])
                for tp in range(16):
                    S.wait(t, "t", gbank_at[tp])
                    for k in range(8):
                        inst = t.matmul(psE[tp % 4][:, 0:NE],
                                        xT_sb[:, k, tp * 128:(tp + 1) * 128],
                                        gw_sb[:, k, :],
                                        start=(k == 0), stop=(k == 7))
                    S.attach(inst, "sT")
                for tp in range(16):
                    S.wait(t, "t", fin_t[tp])
                    S.wait(t, "t", tbank_at[tp])
                    inst = t.transpose(pb45[tp % 2][0:NE, 0:128],
                                       probs_sb[:, tp, :], idf_sb[:, :])
                    S.attach(inst, "sT")

            TB_(gate_t)

            def gate_c(c, gmm=tuple(gmm), nmx_t=tuple(nmx_t)):
                for tp in range(16):
                    a = 8 + 3 * (tp % 2)
                    S.wait(c, "c", nmx_t[tp])
                    i = c.activation(out=pexp_sb[:, tp % 2, :],
                                     in_=psE[tp % 4][:, 0:NE],
                                     func=AF.Exp, bias=stat_sb[:, a:a + 1],
                                     accum_out=stat_sb[:, a + 1:a + 2])
                    S.attach(i, "sS")

            CB_(gate_c)

            def gate_v(v, b=b, gmm=tuple(gmm), gex=tuple(gex), ptr_t=tuple(ptr_t),
                       fin_t=tuple(fin_t)):
                def emit_nmx(tp):
                    a = 8 + 3 * (tp % 2)
                    S.wait(v, "v", gmm[tp])
                    if tp >= 2:
                        S.wait(v, "v", gex[tp - 2])
                    i = v.reduce_max(stat_sb[:, a:a + 1], psE[tp % 4][:, 0:NE],
                                     axis=AX.X, negate=True)
                    S.attach(i, "sV")

                def emit_fin(tp):
                    a = 8 + 3 * (tp % 2)
                    S.wait(v, "v", gex[tp])
                    i = v.reciprocal(stat_sb[:, a + 2:a + 3],
                                     stat_sb[:, a + 1:a + 2])
                    S.attach(i, "sV")
                    v.wait_ge(S.h["sV"], fin_t[tp][1] - 1)
                    i = v.tensor_scalar(out=probs_sb[:, tp, :],
                                        in0=pexp_sb[:, tp % 2, :],
                                        scalar1=stat_sb[:, a + 2:a + 3],
                                        scalar2=None, op0=ALU.mult)
                    S.attach(i, "sV")

                emit_nmx(0)
                for tp in range(1, 16):
                    emit_nmx(tp)
                    emit_fin(tp - 1)
                emit_fin(15)
                for tp in range(16):
                    S.wait(v, "v", ptr_t[tp])
                    inst = v.tensor_copy(
                        P_all[b * 32:b * 32 + 8, tp * 128:(tp + 1) * 128],
                        pb45[tp % 2][0:NE, 0:128])
                    S.attach(inst, "sV")

            VB_(gate_v)
            gate_done_tok[b] = ptr_t[15]
            tok_Pall[b] = pcp_t[15]

            def meta_store(y, b=b):
                S.wait(y, "y", tok_Pall[b])
                S.wait(y, "y", tok_tid)
                dma(y, "y", "dY",
                    xgd_d[b, :, 1024:1040].bitcast(F32).rearrange(
                        "(t p) e -> p t e", p=128),
                    probs_sb[:, :, :])
                dma(y, "y", "dY",
                    xgd_d[b, :, 1040:1042].bitcast(U32).rearrange(
                        "(t p) one -> p t one", p=128),
                    tid_sb[:, :].unsqueeze(2),
                    allow_slow_non_contiguous=True)

            Y_(meta_store)
            tok_xgd_meta[b] = S.cur("dY")

        for cm in [x1r_cm, xbf_cm, x0r_cm, oT_cm, E_cm, v_cm, kvl_cm, kT_cm,
                   qT_cm, xT_cm, wuni_cm]:
            cm.__exit__(None, None, None)

        posu_sb, _ = sb("posu_sb", [128, 16, 64], U32)
        xgt_sb, _ = sb("xgt_sb", [128, 4, XGW], U16)
        xnat_sb, _ = sb("xnat_sb", [128, 2, 1024], BF16)
        w2_sb, _ = sb("w2_sb", [128, 32, D], BF16)
        w1r_sb, w1r_cm = sb("w1r_sb", [128, 2, 8, 1024], BF16)
        hTg_sb, hTg_cm = sb("hTg_sb", [128, 32, 512], BF16)
        xinT_sb, xinT_cm = sb("xinT_sb", [128, 2, 8, 512], BF16)
        meta_sb, meta_cm = sb("meta_sb", [128, 2, 4, 24], U16)
        y_sb, y_cm = sb("y_sb", [128, 1, D], F32)
        W1s, W1_cm = sb("W1s", [64, T], F32)
        W3s, W3_cm = sb("W3s", [64, T], F32)
        W4s = P_all
        W4_cm = None

        ffn_barrier = [S.cur("sT"), S.cur("sV"), S.cur("sS"), S.cur("dY"),
                       S.cur("dY2"), S.cur("dGx")]

        def bisect(v):
            for tk in ffn_barrier:
                S.wait(v, "v", tk)
            S.wait(v, "v", tok_Pall[0])
            S.wait(v, "v", tok_Pall[1])
            S.inc(v.memset(bis_sb[:, 0:1], 0.0), "sV")
            S.inc(v.memset(bis_sb[:, 1:2], 1.0), "sV")
            S.wait(v, "v", S.cur("sV"))
            for it in range(NBIS):
                i = v.tensor_tensor(out=bis_sb[:, 2:3], in0=bis_sb[:, 0:1],
                                    in1=bis_sb[:, 1:2], op=ALU.add)
                S.inc(i, "sV", 1)
                S.wait(v, "v", S.cur("sV"))
                i = v.tensor_scalar(out=bis_sb[:, 2:3], in0=bis_sb[:, 2:3],
                                    scalar1=0.5, scalar2=None, op0=ALU.mult)
                S.inc(i, "sV", 1)
                S.wait(v, "v", S.cur("sV"))
                i = v.tensor_scalar(out=W1s[:, :], in0=P_all[:, :],
                                    scalar1=bis_sb[0:64, 2:3], scalar2=0.0,
                                    op0=ALU.is_gt, op1=ALU.add,
                                    accum_out=bis_sb[0:64, 3:4])
                S.inc(i, "sV", 1)
                S.wait(v, "v", S.cur("sV"))
                i = v.tensor_scalar(out=bis_sb[:, 4:5], in0=bis_sb[:, 3:4],
                                    scalar1=float(KS), scalar2=None, op0=ALU.is_ge)
                S.inc(i, "sV", 1)
                S.wait(v, "v", S.cur("sV"))
                i = v.tensor_tensor(out=bis_sb[:, 6:7], in0=bis_sb[:, 2:3],
                                    in1=bis_sb[:, 0:1], op=ALU.subtract)
                S.inc(i, "sV", 1)
                S.wait(v, "v", S.cur("sV"))
                i = v.tensor_tensor(out=bis_sb[:, 6:7], in0=bis_sb[:, 6:7],
                                    in1=bis_sb[:, 4:5], op=ALU.mult)
                S.inc(i, "sV", 1)
                S.wait(v, "v", S.cur("sV"))
                i = v.tensor_tensor(out=bis_sb[:, 0:1], in0=bis_sb[:, 0:1],
                                    in1=bis_sb[:, 6:7], op=ALU.add)
                S.inc(i, "sV", 1)
                S.wait(v, "v", S.cur("sV"))
                i = v.tensor_scalar(out=bis_sb[:, 4:5], in0=bis_sb[:, 4:5],
                                    scalar1=-1.0, scalar2=1.0,
                                    op0=ALU.mult, op1=ALU.add)
                S.inc(i, "sV", 1)
                S.wait(v, "v", S.cur("sV"))
                i = v.tensor_tensor(out=bis_sb[:, 6:7], in0=bis_sb[:, 2:3],
                                    in1=bis_sb[:, 1:2], op=ALU.subtract)
                S.inc(i, "sV", 1)
                S.wait(v, "v", S.cur("sV"))
                i = v.tensor_tensor(out=bis_sb[:, 6:7], in0=bis_sb[:, 6:7],
                                    in1=bis_sb[:, 4:5], op=ALU.mult)
                S.inc(i, "sV", 1)
                S.wait(v, "v", S.cur("sV"))
                i = v.tensor_tensor(out=bis_sb[:, 1:2], in0=bis_sb[:, 1:2],
                                    in1=bis_sb[:, 6:7], op=ALU.add)
                S.inc(i, "sV", 1)
                S.wait(v, "v", S.cur("sV"))
            i = v.tensor_scalar(out=W1s[:, :], in0=P_all[:, :],
                                scalar1=bis_sb[0:64, 1:2], scalar2=0.0,
                                op0=ALU.is_gt, op1=ALU.add,
                                accum_out=bis_sb[0:64, 3:4])
            S.inc(i, "sV", 1)
            S.wait(v, "v", S.cur("sV"))
            i = v.tensor_scalar(out=bis_sb[:, 5:6], in0=bis_sb[:, 3:4],
                                scalar1=float(KS), scalar2=-1.0,
                                op0=ALU.subtract, op1=ALU.mult)
            S.inc(i, "sV", 1)
            i = v.tensor_scalar(out=W3s[:, :], in0=P_all[:, :],
                                scalar1=bis_sb[0:64, 1:2], scalar2=None,
                                op0=ALU.is_equal)
            S.inc(i, "sV", 1)
            S.wait(v, "v", S.cur("sV"))
            i = v.tensor_tensor_scan(W4s[:, :], W3s[:, :], W3s[:, :], 0.0,
                                     ALU.add, ALU.bypass)
            S.inc(i, "sV", 1)
            S.wait(v, "v", S.cur("sV"))
            i = v.tensor_scalar(out=W4s[:, :], in0=W4s[:, :],
                                scalar1=bis_sb[0:64, 5:6], scalar2=None,
                                op0=ALU.is_le)
            S.inc(i, "sV", 1)
            S.wait(v, "v", S.cur("sV"))
            i = v.tensor_tensor(out=W3s[:, :], in0=W3s[:, :], in1=W4s[:, :],
                                op=ALU.mult)
            S.inc(i, "sV", 1)
            S.wait(v, "v", S.cur("sV"))
            i = v.tensor_tensor(out=W1s[:, :], in0=W1s[:, :], in1=W3s[:, :],
                                op=ALU.add)
            S.inc(i, "sV", 1)
            S.wait(v, "v", S.cur("sV"))
            i = v.tensor_tensor_scan(W4s[:, :], W1s[:, :], W1s[:, :], 0.0,
                                     ALU.add, ALU.bypass)
            S.inc(i, "sV", 1)
            S.wait(v, "v", S.cur("sV"))
            i = v.tensor_tensor(out=W4s[:, :], in0=W4s[:, :], in1=W1s[:, :],
                                op=ALU.subtract)
            S.inc(i, "sV", 1)
            S.wait(v, "v", S.cur("sV"))
            i = v.tensor_scalar(out=W3s[:, :], in0=W1s[:, :], scalar1=-OOBF,
                                scalar2=OOBF, op0=ALU.mult, op1=ALU.add)
            S.inc(i, "sV", 1)
            S.wait(v, "v", S.cur("sV"))
            i = v.tensor_tensor(out=W4s[:, :], in0=W4s[:, :], in1=W3s[:, :],
                                op=ALU.add)
            S.inc(i, "sV", 1)

        V_(bisect)
        posx_tok = S.cur("sV")

        for tp in range(16):
            def pos_tr(t, tp=tp, posx_tok=posx_tok):
                S.wait(t, "t", posx_tok)
                S.wait(t, "t", pv_free[tp % 2])
                inst = t.transpose(pb45[tp % 2][:, 0:64],
                                   W4s[0:64, tp * 128:(tp + 1) * 128],
                                   idf_sb[0:64, 0:64])
                S.inc(inst, "sT", 1)

            T_(pos_tr)
            trt = S.cur("sT")

            def pos_cp(v, tp=tp, trt=trt):
                S.wait(v, "v", trt)
                inst = v.tensor_copy(posu_sb[:, tp, :], pb45[tp % 2][:, 0:64])
                S.inc(inst, "sV", 1)

            V_(pos_cp)
            pv_free[tp % 2] = S.cur("sV")
        posu_tok = S.cur("sV")
        pos_done_tok = S.cur("sT")
        for cm in [W3_cm, W1_cm]:
            cm.__exit__(None, None, None)
        l2a_sb, _ = sb("l2a_sb", [128, 2, D], F32)
        l2b_sb, _ = sb("l2b_sb", [128, 2, D], F32)

        def rezero(v):
            S.inc(v.memset(zsq_sb[:, :], 0.0), "sV")

        V_(rezero)
        rz_tok = S.cur("sV")

        def ffna_fill(z):
            S.wait(z, "z", rz_tok)
            for b in range(NB):
                for tp in range(16):
                    dma(z, "z", "dZ", ffna_d[b][tp * 128:(tp + 1) * 128, :],
                        zsq_sb[:, :])

        C_(ffna_fill)
        ffna_fill_tok = S.cur("dZ")

        NIT = NE * NB
        xgt_free = [None, None, None, None]
        xnat_free = [None, None]
        xin_scat_tok = [None] * NIT
        xinT_tok = [None] * NIT
        meta_tok = [None] * NIT
        xinT_used = [None] * NIT
        meta_used = [None] * NIT
        xnat_ldt = [None] * NIT
        trload_done = [None] * 4
        hT_done = [None] * NIT
        w1p_tok = {}
        w1p_used = {}
        w2_tok = [None] * (NE + 1)
        w2_used = [None] * NE
        ysb_free = [None]
        ln2_ok = False

        def emit_xin_scatter(it):
            e, bb = it // NB, it % NB
            slot = it % 4
            for tp in range(16):
                gpar = tp % 4

                xch = f"dYx{tp % 4}"

                def fl(y, bb=bb, tp=tp, gpar=gpar, xch=xch):
                    S.wait(y, "y", tok_xgd_x1[bb])
                    S.wait(y, "y", tok_xgd_meta[bb])
                    S.wait(y, "y", xgt_free[gpar])
                    dma(y, "y", xch, xgt_sb[:, gpar, :],
                        xgd_d[bb, tp * 128:(tp + 1) * 128, :])

                Y_(fl)
                ldtok = S.cur(xch)

                ch = f"dGs{gpar}"

                def fs(g, e=e, bb=bb, slot=slot, it=it, tp=tp, gpar=gpar,
                       ldtok=ldtok, ch=ch):
                    S.wait(g, "g", posu_tok)
                    S.wait(g, "g", ldtok)
                    if it >= 4 and tp < 4:
                        S.wait(g, "g", trload_done[slot])
                        S.wait(g, "g", meta_tok[it - 4])
                    S.wait(g, "g", S.cur(ch))
                    inst = g.indirect_dma_start(
                        out=xin_d[slot][:, :],
                        out_offset=IndirectOffsetOnAxis(
                            posu_sb[:, tp, bb * 32 + e:bb * 32 + e + 1], 0),
                        in_=xgt_sb[:, gpar, :],
                        in_offset=None,
                        bounds_check=bc_reg[0],
                        oob_is_err=False)
                    S.inc(inst, ch, 16)

                G_(fs)
                xgt_free[gpar] = S.cur(ch)
            xin_scat_tok[it] = [S.cur(f"dGs{i}") for i in range(4)]

        def emit_xnat_loads(it):
            slot = it % 4
            ldt = []
            for sp in range(2):
                npar = (it * 4 + sp) % 2
                nfree = xnat_free[npar]

                def fl(y, slot=slot, it=it, sp=sp, npar=npar, nfree=nfree):
                    for tk in xin_scat_tok[it]:
                        S.wait(y, "y", tk)
                    S.wait(y, "y", nfree)
                    dma(y, "y", "dY", xnat_sb[:, npar, :],
                        xin_d[slot][sp * 128:(sp + 1) * 128, 0:1024].bitcast(BF16))

                Y_(fl)
                ldt.append(S.cur("dY"))
            xnat_ldt[it] = ldt

        def emit_xin_tr(it):
            xpar = it % 2
            slot = it % 4
            trx = [S.reserve("sT") for _ in range(4)]
            evx = [S.reserve("sV") for _ in range(4)]
            ldt = list(xnat_ldt[it])
            for sp in range(2, 4):
                npar = (it * 4 + sp) % 2

                def fl(y, slot=slot, it=it, sp=sp, npar=npar, w=trx[sp - 2]):
                    S.wait(y, "y", w)
                    dma(y, "y", "dY", xnat_sb[:, npar, :],
                        xin_d[slot][sp * 128:(sp + 1) * 128, 0:1024].bitcast(BF16))

                Y_(fl)
                ldt.append(S.cur("dY"))
            trload_done[slot] = ldt[-1]
            psT_at = []
            for j in range(4):
                psT_at.append(psT_free[j % 2])
                psT_free[j % 2] = evx[j]
            xnat_free[0] = trx[2]
            xnat_free[1] = trx[3]

            def ftr(t, it=it, ldt=tuple(ldt), trx=tuple(trx),
                    psT_at=tuple(psT_at)):
                for sp in range(4):
                    npar = (it * 4 + sp) % 2
                    S.wait(t, "t", ldt[sp])
                    S.wait(t, "t", psT_at[sp])
                    for c in range(8):
                        inst = t.transpose(
                            psT[sp % 2][:, c * 128:(c + 1) * 128],
                            xnat_sb[:, npar, c * 128:(c + 1) * 128],
                            idb_sb[:, :])
                    S.attach(inst, "sT")

            TB_(ftr)

            def fev(v, it=it, xpar=xpar, trx=tuple(trx)):
                if it >= 2:
                    S.wait(v, "v", xinT_used[it - 2])
                for sp in range(4):
                    S.wait(v, "v", trx[sp])
                    inst = v.tensor_copy(
                        xinT_sb[:, xpar, :, sp * 128:(sp + 1) * 128],
                        psT[sp % 2][:, :].rearrange("p (a q) -> p a q", a=8))
                    S.attach(inst, "sV")

            VB_(fev)
            xinT_tok[it] = evx[3]

        def emit_xin_meta(it):
            slot = it % 4
            xpar = it % 2

            def fm(y, slot=slot, xpar=xpar, it=it):
                for tk in xin_scat_tok[it]:
                    S.wait(y, "y", tk)
                if it >= 2:
                    S.wait(y, "y", meta_used[it - 2])
                dma(y, "y", "dZ", meta_sb[:, xpar, :, :],
                    xin_d[slot][:, 1024:1048].rearrange("(j p) m -> p j m", p=128))

            Y_(fm)
            meta_tok[it] = S.cur("dZ")

        def emit_w1_piece(gpi):
            it, pf = gpi // 4, gpi % 4
            e = it // NB
            slot = gpi % 2

            wch = f"dYw{pf}"

            def f(g, e=e, pf=pf, slot=slot, gpi=gpi, wch=wch):
                for tk in ffn_barrier:
                    S.wait(g, "g", tk)
                if gpi - 2 in w1p_used and w1p_used[gpi - 2] is not None:
                    g.wait_ge(S.h["sT"], w1p_used[gpi - 2][1])
                dma(g, "g", wch, w1r_sb[:, slot, :, :],
                    w1_d[e, :, pf * 1024:(pf + 1) * 1024].rearrange(
                        "(c p) f -> p c f", p=128))

            G_(f)
            w1p_tok[gpi] = S.cur(wch)

        def emit_w2_load(e):
            def f(g, e=e):
                for tk in ffn_barrier:
                    S.wait(g, "g", tk)
                if e > 0:
                    g.wait_ge(S.h["sT"], w2_used[e - 1][1])
                for piece in range(4):
                    dma(g, "g", f"dYv{piece}",
                        w2_sb[:, piece * 8:(piece + 1) * 8, :],
                        w2_d[e, piece * 1024:(piece + 1) * 1024, :].rearrange(
                            "(c p) d -> p c d", p=128))

            G_(f)
            w2_tok[e] = [S.cur("dYv0"), S.cur("dYv1"),
                         S.cur("dYv2"), S.cur("dYv3")]

        emit_w2_load(0)
        for gpi in range(2):
            emit_w1_piece(gpi)
        emit_xin_scatter(0)
        emit_xin_meta(0)
        emit_xin_scatter(1)
        emit_xin_meta(1)
        emit_xnat_loads(0)
        emit_xin_tr(0)

        prev_scat = None
        last_scat_b = [None, None]
        next_w1 = 2
        for it in range(NIT):
            e, bb = it // NB, it % NB
            xpar = it % 2
            hmm = [S.reserve("sT") for _ in range(32)]
            hge = [S.reserve("sS") for _ in range(32)]
            hent = list(bank_free) + [hge[j - 4] for j in range(4, 32)]
            bank_free = [hge[28], hge[29], hge[30], hge[31]]
            for pf in range(4):
                w1p_used[it * 4 + pf] = hmm[pf * 8 + 7]
            while next_w1 <= it * 4 + 3:
                emit_w1_piece(next_w1)
                next_w1 += 1

            def h_sec(t, it=it, xpar=xpar, hent=tuple(hent)):
                S.wait(t, "t", xinT_tok[it])
                for j in range(32):
                    pf, fc = j // 8, j % 8
                    gpi = it * 4 + pf
                    S.wait(t, "t", w1p_tok[gpi])
                    S.wait(t, "t", hent[j])
                    for k in range(8):
                        inst = t.matmul(
                            psE[j % 4][:, :],
                            w1r_sb[:, gpi % 2, k, fc * 128:(fc + 1) * 128],
                            xinT_sb[:, xpar, k, :],
                            start=(k == 0), stop=(k == 7))
                    S.attach(inst, "sT")

            TB_(h_sec)

            def g_sec(c, hmm=tuple(hmm)):
                for j in range(32):
                    S.wait(c, "c", hmm[j])
                    inst = c.activation(out=hTg_sb[:, j, :],
                                        in_=psE[j % 4][:, :], func=AF.Gelu)
                    S.attach(inst, "sS")

            CB_(g_sec)
            while next_w1 <= min(it * 4 + 5, NIT * 4 - 1):
                emit_w1_piece(next_w1)
                next_w1 += 1
            xinT_used[it] = hmm[31]
            hT_done[it] = hge[31]
            scat_toks = [S.reserve("dGf", 16) for _ in range(4)]
            meta_used[it] = scat_toks[3]
            if it + 1 < NIT:
                emit_xnat_loads(it + 1)
            if it + 2 < NIT:
                emit_xin_scatter(it + 2)
                emit_xin_meta(it + 2)

            ymm = [S.reserve("sT") for _ in range(8)]
            ysc = [S.reserve("sV", 2) for _ in range(4)]
            yent = [pv_free[0], pv_free[1]] + [ysc[(j - 2) // 2] for j in range(2, 8)]
            pv_free[0], pv_free[1] = ysc[3], ysc[3]

            def y_sec(t, it=it, e=e, yent=tuple(yent)):
                S.wait(t, "t", hT_done[it])
                for tk in w2_tok[e]:
                    S.wait(t, "t", tk)
                for j in range(8):
                    sp, dc = j // 2, j % 2
                    S.wait(t, "t", yent[j])
                    for fc in range(32):
                        inst = t.matmul(pb45[j % 2][:, :],
                                        hTg_sb[:, fc, sp * 128:(sp + 1) * 128],
                                        w2_sb[:, fc, dc * 512:(dc + 1) * 512],
                                        start=(fc == 0), stop=(fc == 31))
                    S.attach(inst, "sT")

            TB_(y_sec)

            def yv_sec(v, it=it, e=e, xpar=xpar, ymm=tuple(ymm), ysc=tuple(ysc),
                       yfree=ysb_free[0], scat_toks=tuple(scat_toks)):
                S.wait(v, "v", meta_tok[it])
                for sp in range(4):
                    g_ap = meta_sb[:, xpar, sp, 2 * e:2 * e + 2].bitcast(F32)
                    if sp == 0:
                        S.wait(v, "v", yfree)
                    else:
                        S.wait(v, "v", scat_toks[sp - 1])
                    for dc in range(2):
                        S.wait(v, "v", ymm[sp * 2 + dc])
                        inst = v.tensor_scalar(
                            out=y_sb[:, 0, dc * 512:(dc + 1) * 512],
                            in0=pb45[dc][:, :],
                            scalar1=g_ap, scalar2=None, op0=ALU.mult)
                        S.attach(inst, "sV")

            VB_(yv_sec)

            def y_scat_sec(g, it=it, e=e, bb=bb, xpar=xpar, ysc=tuple(ysc),
                           pscat=prev_scat):
                S.wait(g, "g", ffna_fill_tok)
                if pscat is not None:
                    S.wait(g, "g", pscat)
                for sp in range(4):
                    S.wait(g, "g", ysc[sp])
                    inst = g.indirect_dma_start(
                        out=ffna_d[bb][:, :],
                        out_offset=IndirectOffsetOnAxis(
                            meta_sb[:, xpar, sp, 16:18].bitcast(U32), 0),
                        in_=y_sb[:, 0, :],
                        in_offset=None,
                        compute_op=ALU.add)
                    S.attach(inst, "dGf", 16)
                    if sp < 3:
                        g.wait_ge(S.h["dGf"], scat_toks[sp][1])

            G_(y_scat_sec)
            prev_scat = scat_toks[3]
            ysb_free[0] = scat_toks[3]
            last_scat_b[bb] = scat_toks[3]
            w2_used[e] = ymm[7]
            if it + 1 < NIT:
                emit_xin_tr(it + 1)
            if bb == NB - 1 and e + 1 < NE:
                emit_w2_load(e + 1)
        all_scat = prev_scat

        l2_free = [None, None]
        for bb in range(NB):
            for tp in range(16):
                rpar = tp % 2

                lch = f"dY2{rpar}"

                def l2_load(y, bb=bb, tp=tp, rpar=rpar, lch=lch):
                    S.wait(y, "y", last_scat_b[bb])
                    S.wait(y, "y", pos_done_tok)
                    S.wait(y, "y", l2_free[rpar])
                    dma(y, "y", lch, l2a_sb[:, rpar, :],
                        x1sp_d[bb, tp * 128:(tp + 1) * 128, :])
                    dma(y, "y", lch, l2b_sb[:, rpar, :],
                        ffna_d[bb][tp * 128:(tp + 1) * 128, :])

                Y_(l2_load)
                ldtok = S.cur(lch)

                base = 1 + 7 * (tp % 2)

                def l2_add(v, rpar=rpar, ldtok=ldtok, base=base):
                    S.wait(v, "v", ldtok)
                    tile = l2a_sb[:, rpar, :]
                    i = v.scalar_tensor_tensor(out=tile, in0=l2b_sb[:, rpar, :],
                                               scalar=1.0 / D, in1=tile,
                                               op0=ALU.bypass, op1=ALU.add,
                                               accum_out=stat_sb[:, base:base + 1])
                    S.inc(i, "sV", 1)
                    S.wait(v, "v", S.cur("sV"))
                    i = v.scalar_tensor_tensor(out=zsq_sb[:, :], in0=tile,
                                               scalar=1.0 / D, in1=tile,
                                               op0=ALU.mult, op1=ALU.mult,
                                               accum_out=stat_sb[:, base + 1:base + 2])
                    S.inc(i, "sV", 1)
                    S.wait(v, "v", S.cur("sV"))
                    i = v.tensor_scalar(out=stat_sb[:, base:base + 1],
                                        in0=stat_sb[:, base:base + 1],
                                        scalar1=1.0 / D, scalar2=None, op0=ALU.mult)
                    S.inc(i, "sV", 1)
                    S.wait(v, "v", S.cur("sV"))
                    i = v.tensor_tensor(out=stat_sb[:, base + 2:base + 3],
                                        in0=stat_sb[:, base:base + 1],
                                        in1=stat_sb[:, base:base + 1], op=ALU.mult)
                    S.inc(i, "sV", 1)
                    S.wait(v, "v", S.cur("sV"))
                    i = v.tensor_tensor(out=stat_sb[:, base + 3:base + 4],
                                        in0=stat_sb[:, base + 1:base + 2],
                                        in1=stat_sb[:, base + 2:base + 3],
                                        op=ALU.subtract)
                    S.inc(i, "sV", 1)
                    S.wait(v, "v", S.cur("sV"))
                    emit_var_eps(v, base)

                V_(l2_add)
                vtok = S.cur("sV")
                stok = emit_rsqrt_sec(base, vtok)

                def l2_ap(v, rpar=rpar, base=base, stok=stok):
                    S.wait(v, "v", stok)
                    tile = l2a_sb[:, rpar, :]
                    i = v.tensor_scalar(out=tile, in0=tile,
                                        scalar1=stat_sb[:, base:base + 1],
                                        scalar2=stat_sb[:, base + 5:base + 6],
                                        op0=ALU.subtract, op1=ALU.mult)
                    S.inc(i, "sV", 1)

                V_(l2_ap)
                atok = S.cur("sV")

                def o_store(y, bb=bb, tp=tp, rpar=rpar, atok=atok):
                    S.wait(y, "y", atok)
                    dma(y, "y", "dO", out_d[bb, tp * 128:(tp + 1) * 128, :],
                        l2a_sb[:, rpar, :])

                Y_(o_store)
                l2_free[rpar] = S.cur("dO")

        def fin(y):
            S.wait(y, "y", S.cur("dO"))

        Y_(fin)

    return nc


def kernel(L0_attn_x, L0_attn_Wq, L0_attn_Wkv_down, L0_attn_Wk_up, L0_attn_Wv_up,
           L0_attn_Wo, L0_ffn_gate_w, L0_ffn_w1, L0_ffn_w2):
    x = np.ascontiguousarray(L0_attn_x)
    wqT = np.ascontiguousarray(np.asarray(L0_attn_Wq).T)
    wkvT = np.ascontiguousarray(np.asarray(L0_attn_Wkv_down).T)
    wkT = np.ascontiguousarray(np.asarray(L0_attn_Wk_up).T)
    wvT = np.ascontiguousarray(np.asarray(L0_attn_Wv_up).T)
    woT = np.ascontiguousarray(np.asarray(L0_attn_Wo).T)
    gwT = np.ascontiguousarray(np.asarray(L0_ffn_gate_w).T)
    w1 = np.ascontiguousarray(L0_ffn_w1)
    w2 = np.ascontiguousarray(L0_ffn_w2)
    ident = np.eye(128, dtype=np.float32)

    nc = build()
    in_maps = []
    for c in range(8):
        in_maps.append({
            "x": np.ascontiguousarray(x[2 * c:2 * c + 2]),
            "wqT": wqT, "wkvT": wkvT, "wkT": wkT, "wvT": wvT, "woT": woT,
            "gwT": gwT, "w1": w1, "w2": w2, "ident": ident,
        })
    res = run_bass_kernel_spmd(nc, in_maps, core_ids=list(range(8)))
    out = np.concatenate([r["out"] for r in res.results], axis=0)
    return out.reshape(16, T, D).astype(np.float32)


# version 20 (final)
# speedup vs baseline: 1.1147x; 1.0005x over previous
import numpy as np
import concourse.bass as bass
import concourse.mybir as mybir
from concourse.bass_utils import run_bass_kernel_spmd
from concourse.bass import IndirectOffsetOnAxis

F32 = mybir.dt.float32
BF16 = mybir.dt.bfloat16
U32 = mybir.dt.uint32
U16 = mybir.dt.uint16
AF = mybir.ActivationFunctionType
ALU = mybir.AluOpType
AX = mybir.AxisListType

NB = 2
T = 2048
D = 1024
H = 8
HD = 256
LR = 64
FF = 4096
NE = 8
KS = 512
XGW = 1048
NBIS = 38

OOBF = 1.0e9


class Sems:
    def __init__(self, nc, stack):
        self.h = {}
        self.n = {}
        self.lastw = {}
        self.nc = nc
        self.stack = stack

    def new(self, name):
        self.h[name] = self.stack.enter_context(self.nc.semaphore(name))
        self.n[name] = 0

    def inc(self, inst, name, k=1):
        inst.then_inc(self.h[name], k)
        self.n[name] += k
        return (name, self.n[name])

    def cur(self, name):
        return (name, self.n[name])

    def reserve(self, name, k=1):
        self.n[name] += k
        return (name, self.n[name])

    def attach(self, inst, name, k=1):
        inst.then_inc(self.h[name], k)

    def wait(self, eng, ename, tok):
        if tok is None:
            return
        name, v = tok
        if v <= 0:
            return
        key = (ename, name)
        if self.lastw.get(key, -1) >= v:
            return
        eng.wait_ge(self.h[name], v)
        self.lastw[key] = v


def build(debug=False):
    nc = bass.Bass("TRN2", target_bir_lowering=False)

    x_d = nc.dram_tensor("x", [NB, T, D], F32, kind="ExternalInput").ap()
    wqT_d = nc.dram_tensor("wqT", [D, H * HD], F32, kind="ExternalInput").ap()
    wkvT_d = nc.dram_tensor("wkvT", [D, LR], F32, kind="ExternalInput").ap()
    wkT_d = nc.dram_tensor("wkT", [LR, H * HD], F32, kind="ExternalInput").ap()
    wvT_d = nc.dram_tensor("wvT", [LR, H * HD], F32, kind="ExternalInput").ap()
    woT_d = nc.dram_tensor("woT", [H * HD, D], F32, kind="ExternalInput").ap()
    gwT_d = nc.dram_tensor("gwT", [D, NE], F32, kind="ExternalInput").ap()
    w1_d = nc.dram_tensor("w1", [NE, D, FF], F32, kind="ExternalInput").ap()
    w2_d = nc.dram_tensor("w2", [NE, FF, D], F32, kind="ExternalInput").ap()
    id_d = nc.dram_tensor("ident", [128, 128], F32, kind="ExternalInput").ap()

    out_d = nc.dram_tensor("out", [NB, T, D], F32, kind="ExternalOutput").ap()

    
    xgd_d = nc.dram_tensor("xgd", [NB, T, XGW], U16, kind="Internal").ap()
    x1sp_d = nc.dram_tensor("x1sp", [NB, T, D], F32, kind="Internal").ap()
    ffna_d = [nc.dram_tensor(f"ffna{b}", [T, D], F32, kind="Internal").ap()
              for b in range(NB)]
    xin_d = [nc.dram_tensor(f"xin{s}", [KS, XGW], U16, kind="Internal").ap()
             for s in range(4)]

    from contextlib import ExitStack
    st = ExitStack()
    S = Sems(nc, st)
    for s in ["sT", "sV", "sS", "sG", "dY", "dY2", "dY20", "dY21", "dY22", "dY23", "dYx0", "dYx1", "dYx2", "dYx3", "dYw0", "dYw1", "dYw2", "dYw3", "dYv0", "dYv1", "dYv2", "dYv3", "dZ", "dGw", "dGx", "dGs0", "dGs1", "dGs2", "dGs3", "dGf", "dO", "dO1"]:
        S.new(s)

    def sb(name, shape, dtype):
        cm = nc.sbuf_tensor(name, shape, dtype)
        t = cm.__enter__()
        return t, cm

    def psum(name, shape, dtype):
        cm = nc.psum_tensor(name, shape, dtype)
        t = cm.__enter__()
        return t, cm

    wkv_sb, _ = sb("wkv_sb", [128, 8, LR], BF16)
    wk_sb, _ = sb("wk_sb", [64, H * HD], BF16)
    wv_sb, _ = sb("wv_sb", [64, H * HD], BF16)
    gw_sb, _ = sb("gw_sb", [128, 8, NE], BF16)
    idb_sb, _ = sb("idb_sb", [128, 128], BF16)
    idf_sb, _ = sb("idf_sb", [128, 128], F32)
    zsq_sb, _ = sb("zsq_sb", [128, D], F32)
    stat_sb, _ = sb("stat_sb", [128, 32], F32)
    rc_sb, _ = sb("rc_sb", [128, 2], F32)
    obf_sb, _ = sb("obf_sb", [128, 2, HD], BF16)
    P_all, _ = sb("P_all", [64, T], F32)
    probs_sb, _ = sb("probs_sb", [128, 16, NE], F32)
    pexp_sb, _ = sb("pexp_sb", [128, 2, NE], F32)
    tid_sb, _ = sb("tid_sb", [128, 16], U32)
    bis_sb, _ = sb("bis_sb", [64, 8], F32)

    psE_t, _ = psum("psE", [128, 2048], F32)
    psE = [psE_t[:, q * 512:(q + 1) * 512] for q in range(4)]
    pb4_t, _ = psum("pb4", [128, 512], F32)
    pb5_t, _ = psum("pb5", [128, 512], F32)
    pb45 = [pb4_t, pb5_t]
    psT = []
    for i in range(2):
        t, _ = psum(f"psT{i}", [128, 1024], BF16)
        psT.append(t)

    bank_free = [None] * 4
    pv_free = [None] * 2
    psT_free = [None] * 2

    with nc.Block() as blk:
        def T_(fn):
            blk.tensor(fn)

        def V_(fn):
            pre = S.cur("sV")

            def w(v):
                S.wait(v, "v", pre)
                fn(v)

            blk.vector(w)

        def C_(fn):
            pre = S.cur("sS")

            def w(c):
                S.wait(c, "c", pre)
                fn(c)

            blk.scalar(w)

        def G_(fn):
            blk.gpsimd(fn)

        def Y_(fn):
            blk.sync(fn)

        def TB_(fn):
            blk.tensor(fn)

        def VB_(fn):
            blk.vector(fn)

        def CB_(fn):
            blk.scalar(fn)

        ENAME = {}

        def dma(eng, ename, sem, out, in_, **kw):
            S.wait(eng, ename, S.cur(sem))
            inst = eng.dma_start(out=out, in_=in_, **kw)
            return S.inc(inst, sem, 16)

        def g_dma(g, out, in_, sem, **kw):
            return dma(g, "g", sem, out, in_, **kw)

        bc_reg = [None]

        def setup(g):
            bc_reg[0] = g.alloc_register("bc")
            g.reg_mov(bc_reg[0], KS - 1)
            g_dma(g, wkv_sb[:, :, :], wkvT_d.rearrange("(c p) l -> p c l", p=128), "dGw")
            g_dma(g, wk_sb[:, :], wkT_d[:, :], "dGw")
            g_dma(g, wv_sb[:, :], wvT_d[:, :], "dGw")
            g_dma(g, gw_sb[:, :, :], gwT_d.rearrange("(c p) e -> p c e", p=128), "dGw")
            g_dma(g, idb_sb[:, :], id_d[:, :], "dGw")
            g_dma(g, idf_sb[:, :], id_d[:, :], "dGw")

        G_(setup)
        tok_smallw = S.cur("dGw")

        def setup_g2(g):
            inst = g.iota(tid_sb[:, :], pattern=[[128, 16]], base=0,
                          channel_multiplier=1)
            S.inc(inst, "sG", 1)

        G_(setup_g2)
        tok_tid = S.cur("sG")

        def setup_v(v):
            S.inc(v.memset(zsq_sb[:, :], 0.0), "sV")
            S.inc(v.memset(stat_sb[:, 7:8], 1e-5), "sV")
            S.inc(v.memset(P_all[:, :], 0.0), "sV")

        V_(setup_v)
        tok_zero = S.cur("sV")

        def setup_z(z):
            S.wait(z, "z", tok_zero)
            zb = zsq_sb[:, 0:48].bitcast(U16).rearrange("p (t m) -> p t m", m=6)
            for b in range(NB):
                dma(z, "z", "dZ",
                    xgd_d[b, :, 1042:1048].rearrange("(t p) m -> p t m", p=128),
                    zb)

        C_(setup_z)
        tok_ffna_zero = S.cur("dZ")

        wuni_sb, wuni_cm = sb("wuni_sb", [128, 16384], BF16)
        wq_view = wuni_sb.ap().rearrange("p (c f) -> p c f", c=8)
        wo_view = wuni_sb.ap().rearrange("p (c f) -> p c f", c=16)
        xT_sb, xT_cm = sb("xT_sb", [128, 8, T], BF16)
        qT_sb, qT_cm = sb("qT_sb", [128, 2, T], BF16)
        kT_sb, kT_cm = sb("kT_sb", [128, 2, T], BF16)
        kvl_sb, kvl_cm = sb("kvl_sb", [64, T], BF16)
        v_sb, v_cm = sb("v_sb", [128, 16, HD + 1], BF16)
        E_sb, E_cm = sb("E_sb", [128, 1, 16, 512], BF16)
        oT_sb, oT_cm = sb("oT_sb", [128, 16, T], BF16)
        x0r_sb, x0r_cm = sb("x0r_sb", [128, 1, D], F32)
        xbf_sb, xbf_cm = sb("xbf_sb", [128, 2, D], BF16)
        x1r_sb, x1r_cm = sb("x1r_sb", [128, 1, D], F32)

        def setup_v2(v):
            S.inc(v.memset(v_sb[:, :, HD:HD + 1], 1.0), "sV")

        V_(setup_v2)
        tok_vones = S.cur("sV")

        def wq_load0(g):
            g_dma(g, wq_view, wqT_d.rearrange("(c p) f -> p c f", p=128), "dGw")

        G_(wq_load0)
        tok_wq = [S.cur("dGw"), None]
        tok_wo = [None, None]

        tok_xT = [None, None]
        tok_x1sp = [None, None]
        tok_xgd_x1 = [None, None]
        tok_xgd_meta = [None, None]
        tok_x1T = [None, None]
        tok_Pall = [None, None]

        E_free = [None, None]
        obf_free = [None, None]
        qkv_free = None
        wq_used = [None, None]
        gate_done_tok = [None, None]
        x0r_free = [None, None]
        xbf_war = [None, None]

        xT_psT_par = [0]

        def emit_xT_tile(src_f32_tile_emitter, dst_col, xbf_slot, extra_war=None):
            par = xT_psT_par[0]
            xT_psT_par[0] ^= 1
            ctok = S.reserve("sV")
            trq = S.reserve("sT")
            evq = S.reserve("sV")
            psT_at = psT_free[par]
            psT_free[par] = evq

            def vcast(v):
                src_f32_tile_emitter(v)

            def ftr(t, psT_at=psT_at, ctok=ctok, par=par, xbf_slot=xbf_slot):
                S.wait(t, "t", tok_smallw)
                S.wait(t, "t", ctok)
                S.wait(t, "t", psT_at)
                for c in range(8):
                    inst = t.transpose(psT[par][:, c * 128:(c + 1) * 128],
                                       xbf_sb[:, xbf_slot, c * 128:(c + 1) * 128],
                                       idb_sb[:, :])
                S.attach(inst, "sT")

            def fev(v, trq=trq, dst_col=dst_col, extra_war=extra_war, par=par):
                if extra_war is not None:
                    S.wait(v, "v", extra_war)
                S.wait(v, "v", trq)
                inst = v.tensor_copy(
                    xT_sb[:, :, dst_col:dst_col + 128],
                    psT[par][:, :].rearrange("p (a q) -> p a q", a=8))
                S.attach(inst, "sV")

            VB_(vcast)
            TB_(ftr)
            VB_(fev)
            return ctok, trq, evq


        def emit_var_eps(v, base):
            va = stat_sb[:, base + 3:base + 4]
            i = v.tensor_scalar(out=va, in0=va, scalar1=1e-5, scalar2=None,
                                op0=ALU.add)
            S.inc(i, "sV", 1)
            S.wait(v, "v", S.cur("sV"))
            i = v.reciprocal(stat_sb[:, base + 4:base + 5], va)
            S.inc(i, "sV", 1)

        def emit_rsqrt_sec(base, vtok):
            def rs(c, base=base, vtok=vtok):
                S.wait(c, "c", vtok)
                i = c.activation(out=stat_sb[:, base + 5:base + 6],
                                 in_=stat_sb[:, base + 4:base + 5],
                                 func=AF.Sqrt)
                S.inc(i, "sS", 1)

            C_(rs)
            return S.cur("sS")

        for b in range(NB):
            for tp in range(16):
                rp = tp % 2

                def x0_load(y, b=b, tp=tp):
                    S.wait(y, "y", x0r_free[0])
                    dma(y, "y", "dY2", x0r_sb[:, 0, :],
                        x_d[b, tp * 128:(tp + 1) * 128, :])

                Y_(x0_load)
                ldtok = S.cur("dY2")

                def cast_em(v, ldtok=ldtok, xw=xbf_war[rp], rp=rp):
                    S.wait(v, "v", ldtok)
                    if xw is not None:
                        S.wait(v, "v", xw)
                    inst = v.tensor_copy(xbf_sb[:, rp, :], x0r_sb[:, 0, :])
                    S.attach(inst, "sV")

                ctok, trq, evq = emit_xT_tile(
                    cast_em, tp * 128, rp,
                    gate_done_tok[0] if b == 1 else None)
                xbf_war[rp] = trq
                x0r_free[0] = ctok
            tok_xT[b] = evq

            if b == 1:
                def wq_load1(g):
                    g.wait_ge(S.h["sT"], gate_done_tok[0][1])
                    g_dma(g, wq_view,
                          wqT_d.rearrange("(c p) f -> p c f", p=128), "dGw")

                G_(wq_load1)
                tok_wq[1] = S.cur("dGw")


            for h in range(H):
                jobs = []
                if h == 0:
                    jobs += [("kvl", qc) for qc in range(4)]
                jobs += [("q", (m, qc)) for m in range(2) for qc in range(4)]
                jobs += [("k", (m, qc)) for m in range(2) for qc in range(4)]
                jobs += [("v", kp) for kp in range(16)]
                pmm = [S.reserve("sT") for _ in jobs]
                pev = [S.reserve("sV") for _ in jobs]
                ent_at = []
                ent = list(bank_free)
                for i in range(len(jobs)):
                    ent_at.append(ent[i % 4])
                    ent[i % 4] = pev[i]
                bank_free = ent
                nkvl = 4 if h == 0 else 0
                if h == 0:
                    kvl_tok = pev[3]
                v_tok = pev[-1]

                def proj_sec(t, b=b, h=h, jobs=tuple(jobs), ent_at=tuple(ent_at),
                             qf=qkv_free, kvt=kvl_tok):
                    S.wait(t, "t", tok_smallw)
                    S.wait(t, "t", tok_wq[b])
                    S.wait(t, "t", tok_xT[b])
                    if qf is not None:
                        S.wait(t, "t", qf)
                    for i, (kind, pr) in enumerate(jobs):
                        S.wait(t, "t", ent_at[i])
                        bk = i % 4
                        if kind == "kvl":
                            qc = pr
                            for k in range(8):
                                inst = t.matmul(psE[bk][0:64, :], wkv_sb[:, k, :],
                                                xT_sb[:, k, qc * 512:(qc + 1) * 512],
                                                start=(k == 0), stop=(k == 7))
                        elif kind == "q":
                            m, qc = pr
                            for k in range(8):
                                inst = t.matmul(
                                    psE[bk][:, :],
                                    wq_view[:, k, h * 256 + m * 128: h * 256 + (m + 1) * 128],
                                    xT_sb[:, k, qc * 512:(qc + 1) * 512],
                                    start=(k == 0), stop=(k == 7))
                        elif kind == "k":
                            m, qc = pr
                            S.wait(t, "t", kvt)
                            inst = t.matmul(
                                psE[bk][:, :],
                                wk_sb[0:64, h * 256 + m * 128: h * 256 + (m + 1) * 128],
                                kvl_sb[0:64, qc * 512:(qc + 1) * 512],
                                start=True, stop=True)
                        else:
                            kp = pr
                            S.wait(t, "t", kvt)
                            inst = t.matmul(psE[bk][:, 0:HD],
                                            kvl_sb[0:64, kp * 128:(kp + 1) * 128],
                                            wv_sb[0:64, h * 256:(h + 1) * 256],
                                            start=True, stop=True)
                        S.attach(inst, "sT")

                TB_(proj_sec)

                def proj_ev(v, h=h, jobs=tuple(jobs), pmm=tuple(pmm),
                            pev=tuple(pev)):
                    S.wait(v, "v", tok_vones)
                    for i, (kind, pr) in enumerate(jobs):
                        S.wait(v, "v", pmm[i])
                        bk = i % 4
                        if kind == "kvl":
                            qc = pr
                            inst = v.tensor_copy(
                                kvl_sb[0:64, qc * 512:(qc + 1) * 512],
                                psE[bk][0:64, :])
                        elif kind == "q":
                            m, qc = pr
                            inst = v.tensor_copy(
                                qT_sb[:, m, qc * 512:(qc + 1) * 512], psE[bk][:, :])
                        elif kind == "k":
                            m, qc = pr
                            inst = v.tensor_copy(
                                kT_sb[:, m, qc * 512:(qc + 1) * 512], psE[bk][:, :])
                        else:
                            kp = pr
                            inst = v.tensor_copy(v_sb[:, kp, 0:HD], psE[bk][:, 0:HD])
                        S.attach(inst, "sV")

                VB_(proj_ev)
                wq_used[b] = pmm[nkvl + 7]

                for qc in range(4):
                    smm = [S.reserve("sT") for _ in range(16)]
                    sexp = [S.reserve("sS") for _ in range(16)]
                    t_order = ["pv0", "pv1", "tr0", "pv2", "tr1", "pv3", "tr2", "tr3"]
                    ptk = {nm: S.reserve("sT") for nm in t_order}
                    v_order = ["nrm0", "nrm1", "ocp0", "nrm2", "ocp1", "nrm3",
                               "ocp2", "ocp3"]
                    vtk = {}
                    for nm in v_order:
                        vtk[nm] = S.reserve("sV", 2 if nm.startswith("nrm") else 1)
                    sent = list(bank_free)
                    sent_at = []
                    for kp in range(16):
                        sent_at.append(sent[kp % 4])
                        sent[kp % 4] = sexp[kp]
                    bank_free = sent
                    pv_ent = [pv_free[0], pv_free[1]]
                    pv_at = []
                    for qp in range(4):
                        pv_at.append(pv_ent[qp % 2])
                        pv_ent[qp % 2] = vtk[f"nrm{qp}"]
                    pv_free[0], pv_free[1] = pv_ent[0], pv_ent[1]
                    psT_at = [psT_free[0], psT_free[1]]
                    obf_at = [obf_free[0], obf_free[1]]
                    for qp in range(4):
                        pp = qp % 2
                        psT_at.append(psT_free[pp])
                        obf_at.append(obf_free[pp])
                        psT_free[pp] = vtk[f"ocp{qp}"]
                        obf_free[pp] = ptk[f"tr{qp}"]
                    psT_at = psT_at[2:]
                    obf_at = obf_at[2:]

                    def sc_sec(t, h=h, qc=qc, smm=tuple(smm), sexp=tuple(sexp),
                               sent_at=tuple(sent_at), pv_at=tuple(pv_at),
                               psT_at=tuple(psT_at), ptk=dict(ptk), vtk=dict(vtk),
                               efree=E_free[0], v_tok=v_tok):
                        S.wait(t, "t", efree)
                        for kp in range(16):
                            S.wait(t, "t", sent_at[kp])
                            bk = kp % 4
                            for m in range(2):
                                inst = t.matmul(psE[bk][:, :],
                                                kT_sb[:, m, kp * 128:(kp + 1) * 128],
                                                qT_sb[:, m, qc * 512:(qc + 1) * 512],
                                                start=(m == 0), stop=(m == 1))
                            S.attach(inst, "sT")

                        def emit_pv(qp):
                            bk = qp % 2
                            S.wait(t, "t", sexp[15])
                            S.wait(t, "t", v_tok)
                            S.wait(t, "t", pv_at[qp])
                            for kp in range(16):
                                inst = t.matmul(
                                    pb45[bk][:, 0:HD + 1],
                                    E_sb[:, 0, kp, qp * 128:(qp + 1) * 128],
                                    v_sb[:, kp, :],
                                    start=(kp == 0), stop=(kp == 15))
                            S.attach(inst, "sT")

                        def emit_tr(qp):
                            par = qp % 2
                            S.wait(t, "t", vtk[f"nrm{qp}"])
                            S.wait(t, "t", psT_at[qp])
                            t.transpose(psT[par][:, 0:128], obf_sb[:, par, 0:128],
                                        idb_sb[:, :])
                            inst = t.transpose(psT[par][:, 128:256],
                                               obf_sb[:, par, 128:256],
                                               idb_sb[:, :])
                            S.attach(inst, "sT")

                        emit_pv(0)
                        emit_pv(1)
                        emit_tr(0)
                        emit_pv(2)
                        emit_tr(1)
                        emit_pv(3)
                        emit_tr(2)
                        emit_tr(3)

                    TB_(sc_sec)

                    def exp_sec(c, qc=qc, smm=tuple(smm)):
                        for kp in range(16):
                            S.wait(c, "c", smm[kp])
                            inst = c.activation(out=E_sb[:, 0, kp, :],
                                                in_=psE[kp % 4][:, :],
                                                func=AF.Exp, scale=1.0 / 16.0)
                            S.attach(inst, "sS")

                    CB_(exp_sec)

                    def pvv_sec(v, h=h, qc=qc, ptk=dict(ptk), vtk=dict(vtk),
                                obf_at=tuple(obf_at)):
                        def emit_nrm(qp):
                            bk = qp % 2
                            par = qp % 2
                            S.wait(v, "v", ptk[f"pv{qp}"])
                            S.wait(v, "v", obf_at[qp])
                            i1 = v.reciprocal(rc_sb[:, par:par + 1],
                                              pb45[bk][:, HD:HD + 1])
                            S.attach(i1, "sV")
                            v.wait_ge(S.h["sV"], vtk[f"nrm{qp}"][1] - 1)
                            i2 = v.tensor_scalar(out=obf_sb[:, par, :],
                                                 in0=pb45[bk][:, 0:HD],
                                                 scalar1=rc_sb[:, par:par + 1],
                                                 scalar2=None, op0=ALU.mult)
                            S.attach(i2, "sV")

                        def emit_ocp(qp):
                            par = qp % 2
                            qglob = qc * 512 + qp * 128
                            S.wait(v, "v", ptk[f"tr{qp}"])
                            inst = v.tensor_copy(
                                oT_sb[:, 2 * h:2 * h + 2, qglob:qglob + 128],
                                psT[par][:, 0:256].rearrange("p (a q) -> p a q", a=2))
                            S.attach(inst, "sV")

                        emit_nrm(0)
                        emit_nrm(1)
                        emit_ocp(0)
                        emit_nrm(2)
                        emit_ocp(1)
                        emit_nrm(3)
                        emit_ocp(2)
                        emit_ocp(3)

                    VB_(pvv_sec)
                    E_free[0] = ptk["pv3"]
                qkv_free = ptk["tr3"]
            oT_ready = S.cur("sV")

            def wo_load(g, b=b):
                g.wait_ge(S.h["sT"], wq_used[b][1])
                g_dma(g, wo_view, woT_d.rearrange("(c p) f -> p c f", p=128), "dGw")

            G_(wo_load)
            tok_wo[b] = S.cur("dGw")

            x1r_free = [None]
            x1r_free2 = None
            for qp in range(16):
                rpar = 0

                def x0_pref(y, b=b, qp=qp):
                    S.wait(y, "y", x0r_free[0])
                    dma(y, "y", "dY2", x0r_sb[:, 0, :],
                        x_d[b, qp * 128:(qp + 1) * 128, :])

                Y_(x0_pref)
                x0tok = S.cur("dY2")

                mmtoks = []
                for dc in range(2):
                    bk = dc

                    def wo_mm(t, b=b, qp=qp, dc=dc, bk=bk, oT_ready=oT_ready):
                        S.wait(t, "t", oT_ready)
                        S.wait(t, "t", tok_wo[b])
                        S.wait(t, "t", pv_free[bk])
                        for kc in range(16):
                            inst = t.matmul(pb45[bk][:, :],
                                            oT_sb[:, kc, qp * 128:(qp + 1) * 128],
                                            wo_view[:, kc, dc * 512:(dc + 1) * 512],
                                            start=(kc == 0), stop=(kc == 15))
                        S.inc(inst, "sT", 1)

                    T_(wo_mm)
                    mmtoks.append(S.cur("sT"))

                def resid(v, qp=qp, rpar=rpar, x0tok=x0tok, mmtoks=tuple(mmtoks)):
                    S.wait(v, "v", x0tok)
                    S.wait(v, "v", x1r_free[rpar])
                    S.wait(v, "v", x1r_free2)
                    for dc in range(2):
                        S.wait(v, "v", mmtoks[dc])
                        inst = v.tensor_tensor(out=x1r_sb[:, rpar, dc * 512:(dc + 1) * 512],
                                               in0=pb45[dc][:, :],
                                               in1=x0r_sb[:, 0, dc * 512:(dc + 1) * 512],
                                               op=ALU.add)
                        S.inc(inst, "sV", 1)

                V_(resid)
                pv_free[0] = S.cur("sV")
                pv_free[1] = S.cur("sV")
                x0r_free[0] = S.cur("sV")
                rtok = S.cur("sV")

                base = 1 + 7 * (qp % 2)

                def ln1(v, rpar=rpar, rtok=rtok, base=base):
                    S.wait(v, "v", rtok)
                    S.wait(v, "v", tok_ffna_zero)
                    tile = x1r_sb[:, rpar, :]
                    i = v.tensor_scalar(out=zsq_sb[:, :], in0=tile, scalar1=1.0 / D,
                                        scalar2=0.0, op0=ALU.mult, op1=ALU.add,
                                        accum_out=stat_sb[:, base:base + 1])
                    S.inc(i, "sV", 1)
                    S.wait(v, "v", S.cur("sV"))
                    i = v.scalar_tensor_tensor(out=zsq_sb[:, :], in0=tile,
                                               scalar=1.0 / D, in1=tile,
                                               op0=ALU.mult, op1=ALU.mult,
                                               accum_out=stat_sb[:, base + 1:base + 2])
                    S.inc(i, "sV", 1)
                    S.wait(v, "v", S.cur("sV"))
                    i = v.tensor_tensor(out=stat_sb[:, base + 2:base + 3],
                                        in0=stat_sb[:, base:base + 1],
                                        in1=stat_sb[:, base:base + 1], op=ALU.mult)
                    S.inc(i, "sV", 1)
                    S.wait(v, "v", S.cur("sV"))
                    i = v.tensor_tensor(out=stat_sb[:, base + 3:base + 4],
                                        in0=stat_sb[:, base + 1:base + 2],
                                        in1=stat_sb[:, base + 2:base + 3],
                                        op=ALU.subtract)
                    S.inc(i, "sV", 1)
                    S.wait(v, "v", S.cur("sV"))
                    emit_var_eps(v, base)

                V_(ln1)
                vtok = S.cur("sV")
                stok = emit_rsqrt_sec(base, vtok)

                def ln1_ap(v, rpar=rpar, base=base, stok=stok):
                    S.wait(v, "v", stok)
                    tile = x1r_sb[:, rpar, :]
                    i = v.tensor_scalar(out=tile, in0=tile,
                                        scalar1=stat_sb[:, base:base + 1],
                                        scalar2=stat_sb[:, base + 5:base + 6],
                                        op0=ALU.subtract, op1=ALU.mult)
                    S.inc(i, "sV", 1)

                V_(ln1_ap)
                atok = S.cur("sV")

                def x1_store(y, b=b, qp=qp, rpar=rpar, atok=atok):
                    S.wait(y, "y", atok)
                    dma(y, "y", "dY", x1sp_d[b, qp * 128:(qp + 1) * 128, :],
                        x1r_sb[:, rpar, :])

                Y_(x1_store)
                store_tok = S.cur("dY")

                xrp = qp % 2

                def cast1_em(v, atok=atok, xw=xbf_war[xrp], xrp=xrp):
                    S.wait(v, "v", atok)
                    if xw is not None:
                        S.wait(v, "v", xw)
                    inst = v.tensor_copy(xbf_sb[:, xrp, :], x1r_sb[:, 0, :])
                    S.attach(inst, "sV")

                ctok, trq, evq = emit_xT_tile(cast1_em, qp * 128, xrp)
                xbf_war[xrp] = trq
                x1r_free[rpar] = store_tok
                x1r_free2 = ctok
            tok_x1sp[b] = S.cur("dY")
            tok_x1T[b] = evq



            def xg_cast(g, b=b):
                g.wait_ge(S.h["dY"], tok_x1sp[b][1])
                g_dma(g, xgd_d[b, :, 0:1024].bitcast(BF16), x1sp_d[b, :, :], "dGx")

            G_(xg_cast)
            tok_xgd_x1[b] = S.cur("dGx")

            gmm = [S.reserve("sT") for _ in range(16)]
            nmx_t = [None] * 16
            fin_t = [None] * 16
            nmx_t[0] = S.reserve("sV")
            for tp in range(1, 16):
                nmx_t[tp] = S.reserve("sV")
                fin_t[tp - 1] = S.reserve("sV", 2)
            fin_t[15] = S.reserve("sV", 2)
            gex = [S.reserve("sS") for _ in range(16)]
            ptr_t = [S.reserve("sT") for _ in range(16)]
            pcp_t = [S.reserve("sV") for _ in range(16)]
            gbank_at = list(bank_free) + [gex[tp - 4] for tp in range(4, 16)]
            tbank_at = [pv_free[0], pv_free[1]] + [pcp_t[tp - 2] for tp in range(2, 16)]
            bank_free = [gex[12], gex[13], gex[14], gex[15]]
            pv_free[0], pv_free[1] = pcp_t[14], pcp_t[15]

            def gate_t(t, b=b, gbank_at=tuple(gbank_at), tbank_at=tuple(tbank_at),
                       fin_t=tuple(fin_t)):
                S.wait(t, "t", tok_x1T[b])
                for tp in range(16):
                    S.wait(t, "t", gbank_at[tp])
                    for k in range(8):
                        inst = t.matmul(psE[tp % 4][:, 0:NE],
                                        xT_sb[:, k, tp * 128:(tp + 1) * 128],
                                        gw_sb[:, k, :],
                                        start=(k == 0), stop=(k == 7))
                    S.attach(inst, "sT")
                for tp in range(16):
                    S.wait(t, "t", fin_t[tp])
                    S.wait(t, "t", tbank_at[tp])
                    inst = t.transpose(pb45[tp % 2][0:NE, 0:128],
                                       probs_sb[:, tp, :], idf_sb[:, :])
                    S.attach(inst, "sT")

            TB_(gate_t)

            def gate_c(c, gmm=tuple(gmm), nmx_t=tuple(nmx_t)):
                for tp in range(16):
                    a = 8 + 3 * (tp % 2)
                    S.wait(c, "c", nmx_t[tp])
                    i = c.activation(out=pexp_sb[:, tp % 2, :],
                                     in_=psE[tp % 4][:, 0:NE],
                                     func=AF.Exp, bias=stat_sb[:, a:a + 1],
                                     accum_out=stat_sb[:, a + 1:a + 2])
                    S.attach(i, "sS")

            CB_(gate_c)

            def gate_v(v, b=b, gmm=tuple(gmm), gex=tuple(gex), ptr_t=tuple(ptr_t),
                       fin_t=tuple(fin_t)):
                def emit_nmx(tp):
                    a = 8 + 3 * (tp % 2)
                    S.wait(v, "v", gmm[tp])
                    if tp >= 2:
                        S.wait(v, "v", gex[tp - 2])
                    i = v.reduce_max(stat_sb[:, a:a + 1], psE[tp % 4][:, 0:NE],
                                     axis=AX.X, negate=True)
                    S.attach(i, "sV")

                def emit_fin(tp):
                    a = 8 + 3 * (tp % 2)
                    S.wait(v, "v", gex[tp])
                    i = v.reciprocal(stat_sb[:, a + 2:a + 3],
                                     stat_sb[:, a + 1:a + 2])
                    S.attach(i, "sV")
                    v.wait_ge(S.h["sV"], fin_t[tp][1] - 1)
                    i = v.tensor_scalar(out=probs_sb[:, tp, :],
                                        in0=pexp_sb[:, tp % 2, :],
                                        scalar1=stat_sb[:, a + 2:a + 3],
                                        scalar2=None, op0=ALU.mult)
                    S.attach(i, "sV")

                emit_nmx(0)
                for tp in range(1, 16):
                    emit_nmx(tp)
                    emit_fin(tp - 1)
                emit_fin(15)
                for tp in range(16):
                    S.wait(v, "v", ptr_t[tp])
                    inst = v.tensor_copy(
                        P_all[b * 32:b * 32 + 8, tp * 128:(tp + 1) * 128],
                        pb45[tp % 2][0:NE, 0:128])
                    S.attach(inst, "sV")

            VB_(gate_v)
            gate_done_tok[b] = ptr_t[15]
            tok_Pall[b] = pcp_t[15]

            def meta_store(y, b=b):
                S.wait(y, "y", tok_Pall[b])
                S.wait(y, "y", tok_tid)
                dma(y, "y", "dY",
                    xgd_d[b, :, 1024:1040].bitcast(F32).rearrange(
                        "(t p) e -> p t e", p=128),
                    probs_sb[:, :, :])
                dma(y, "y", "dY",
                    xgd_d[b, :, 1040:1042].bitcast(U32).rearrange(
                        "(t p) one -> p t one", p=128),
                    tid_sb[:, :].unsqueeze(2),
                    allow_slow_non_contiguous=True)

            Y_(meta_store)
            tok_xgd_meta[b] = S.cur("dY")

        for cm in [x1r_cm, xbf_cm, x0r_cm, oT_cm, E_cm, v_cm, kvl_cm, kT_cm,
                   qT_cm, xT_cm, wuni_cm]:
            cm.__exit__(None, None, None)

        posu_sb, _ = sb("posu_sb", [128, 16, 64], U32)
        xgt_sb, _ = sb("xgt_sb", [128, 4, XGW], U16)
        xnat_sb, _ = sb("xnat_sb", [128, 2, 1024], BF16)
        w2_sb, _ = sb("w2_sb", [128, 32, D], BF16)
        w1r_sb, w1r_cm = sb("w1r_sb", [128, 2, 8, 1024], BF16)
        hTg_sb, hTg_cm = sb("hTg_sb", [128, 32, 512], BF16)
        xinT_sb, xinT_cm = sb("xinT_sb", [128, 2, 8, 512], BF16)
        meta_sb, meta_cm = sb("meta_sb", [128, 2, 4, 24], U16)
        y_sb, y_cm = sb("y_sb", [128, 1, D], F32)
        W1s, W1_cm = sb("W1s", [64, T], F32)
        W3s, W3_cm = sb("W3s", [64, T], F32)
        W4s = P_all
        W4_cm = None

        ffn_barrier = [S.cur("sT"), S.cur("sV"), S.cur("sS"), S.cur("dY"),
                       S.cur("dY2"), S.cur("dGx")]

        def bisect(v):
            for tk in ffn_barrier:
                S.wait(v, "v", tk)
            S.wait(v, "v", tok_Pall[0])
            S.wait(v, "v", tok_Pall[1])
            S.inc(v.memset(bis_sb[:, 0:1], 0.0), "sV")
            S.inc(v.memset(bis_sb[:, 1:2], 1.0), "sV")
            ini = S.cur("sV")
            rb = [slice(0, 32), slice(32, 64)]
            lo_t = [ini, ini]
            hi_t = [ini, ini]
            mid_t = [None, None]
            cnt_t = [None, None]
            ge_t = [None, None]
            gl_t = [None, None]
            dlo_t = [None, None]
            dhi_t = [None, None]
            for it in range(NBIS):
                for b in range(2):
                    r = rb[b]
                    S.wait(v, "v", lo_t[b])
                    S.wait(v, "v", hi_t[b])
                    i = v.tensor_scalar(out=bis_sb[r, 2:3], in0=bis_sb[r, 0:1],
                                        scalar1=bis_sb[r, 1:2], scalar2=0.5,
                                        op0=ALU.add, op1=ALU.mult)
                    mid_t[b] = S.inc(i, "sV", 1)
                for b in range(2):
                    r = rb[b]
                    S.wait(v, "v", mid_t[b])
                    i = v.tensor_scalar(out=W1s[r, :], in0=P_all[r, :],
                                        scalar1=bis_sb[r, 2:3], scalar2=0.0,
                                        op0=ALU.is_gt, op1=ALU.add,
                                        accum_out=bis_sb[r, 3:4])
                    cnt_t[b] = S.inc(i, "sV", 1)
                for b in range(2):
                    r = rb[b]
                    S.wait(v, "v", cnt_t[b])
                    i = v.tensor_scalar(out=bis_sb[r, 4:5], in0=bis_sb[r, 3:4],
                                        scalar1=float(KS), scalar2=None,
                                        op0=ALU.is_ge)
                    ge_t[b] = S.inc(i, "sV", 1)
                for b in range(2):
                    r = rb[b]
                    S.wait(v, "v", cnt_t[b])
                    i = v.tensor_scalar(out=bis_sb[r, 5:6], in0=bis_sb[r, 3:4],
                                        scalar1=float(KS), scalar2=None,
                                        op0=ALU.is_lt)
                    gl_t[b] = S.inc(i, "sV", 1)
                for b in range(2):
                    r = rb[b]
                    S.wait(v, "v", ge_t[b])
                    i = v.scalar_tensor_tensor(out=bis_sb[r, 6:7],
                                               in0=bis_sb[r, 2:3],
                                               scalar=bis_sb[r, 0:1],
                                               in1=bis_sb[r, 4:5],
                                               op0=ALU.subtract, op1=ALU.mult)
                    dlo_t[b] = S.inc(i, "sV", 1)
                for b in range(2):
                    r = rb[b]
                    S.wait(v, "v", gl_t[b])
                    i = v.scalar_tensor_tensor(out=bis_sb[r, 7:8],
                                               in0=bis_sb[r, 2:3],
                                               scalar=bis_sb[r, 1:2],
                                               in1=bis_sb[r, 5:6],
                                               op0=ALU.subtract, op1=ALU.mult)
                    dhi_t[b] = S.inc(i, "sV", 1)
                for b in range(2):
                    r = rb[b]
                    S.wait(v, "v", dlo_t[b])
                    i = v.tensor_tensor(out=bis_sb[r, 0:1], in0=bis_sb[r, 0:1],
                                        in1=bis_sb[r, 6:7], op=ALU.add)
                    lo_t[b] = S.inc(i, "sV", 1)
                for b in range(2):
                    r = rb[b]
                    S.wait(v, "v", dhi_t[b])
                    i = v.tensor_tensor(out=bis_sb[r, 1:2], in0=bis_sb[r, 1:2],
                                        in1=bis_sb[r, 7:8], op=ALU.add)
                    hi_t[b] = S.inc(i, "sV", 1)
            S.wait(v, "v", hi_t[0])
            S.wait(v, "v", hi_t[1])
            i = v.tensor_scalar(out=W1s[:, :], in0=P_all[:, :],
                                scalar1=bis_sb[0:64, 1:2], scalar2=0.0,
                                op0=ALU.is_gt, op1=ALU.add,
                                accum_out=bis_sb[0:64, 3:4])
            S.inc(i, "sV", 1)
            S.wait(v, "v", S.cur("sV"))
            i = v.tensor_scalar(out=bis_sb[:, 5:6], in0=bis_sb[:, 3:4],
                                scalar1=float(KS), scalar2=-1.0,
                                op0=ALU.subtract, op1=ALU.mult)
            S.inc(i, "sV", 1)
            i = v.tensor_scalar(out=W3s[:, :], in0=P_all[:, :],
                                scalar1=bis_sb[0:64, 1:2], scalar2=None,
                                op0=ALU.is_equal)
            S.inc(i, "sV", 1)
            S.wait(v, "v", S.cur("sV"))
            i = v.tensor_tensor_scan(W4s[:, :], W3s[:, :], W3s[:, :], 0.0,
                                     ALU.add, ALU.bypass)
            S.inc(i, "sV", 1)
            S.wait(v, "v", S.cur("sV"))
            i = v.tensor_scalar(out=W4s[:, :], in0=W4s[:, :],
                                scalar1=bis_sb[0:64, 5:6], scalar2=None,
                                op0=ALU.is_le)
            S.inc(i, "sV", 1)
            S.wait(v, "v", S.cur("sV"))
            i = v.tensor_tensor(out=W3s[:, :], in0=W3s[:, :], in1=W4s[:, :],
                                op=ALU.mult)
            S.inc(i, "sV", 1)
            S.wait(v, "v", S.cur("sV"))
            i = v.tensor_tensor(out=W1s[:, :], in0=W1s[:, :], in1=W3s[:, :],
                                op=ALU.add)
            S.inc(i, "sV", 1)
            S.wait(v, "v", S.cur("sV"))
            i = v.tensor_tensor_scan(W4s[:, :], W1s[:, :], W1s[:, :], 0.0,
                                     ALU.add, ALU.bypass)
            S.inc(i, "sV", 1)
            S.wait(v, "v", S.cur("sV"))
            i = v.tensor_tensor(out=W4s[:, :], in0=W4s[:, :], in1=W1s[:, :],
                                op=ALU.subtract)
            S.inc(i, "sV", 1)
            S.wait(v, "v", S.cur("sV"))
            i = v.tensor_scalar(out=W3s[:, :], in0=W1s[:, :], scalar1=-OOBF,
                                scalar2=OOBF, op0=ALU.mult, op1=ALU.add)
            S.inc(i, "sV", 1)
            S.wait(v, "v", S.cur("sV"))
            i = v.tensor_tensor(out=W4s[:, :], in0=W4s[:, :], in1=W3s[:, :],
                                op=ALU.add)
            S.inc(i, "sV", 1)

        V_(bisect)
        posx_tok = S.cur("sV")

        for tp in range(16):
            def pos_tr(t, tp=tp, posx_tok=posx_tok):
                S.wait(t, "t", posx_tok)
                S.wait(t, "t", pv_free[tp % 2])
                inst = t.transpose(pb45[tp % 2][:, 0:64],
                                   W4s[0:64, tp * 128:(tp + 1) * 128],
                                   idf_sb[0:64, 0:64])
                S.inc(inst, "sT", 1)

            T_(pos_tr)
            trt = S.cur("sT")

            def pos_cp(v, tp=tp, trt=trt):
                S.wait(v, "v", trt)
                inst = v.tensor_copy(posu_sb[:, tp, :], pb45[tp % 2][:, 0:64])
                S.inc(inst, "sV", 1)

            V_(pos_cp)
            pv_free[tp % 2] = S.cur("sV")
        posu_tok = S.cur("sV")
        pos_done_tok = S.cur("sT")
        for cm in [W3_cm, W1_cm]:
            cm.__exit__(None, None, None)
        l2a_sb, _ = sb("l2a_sb", [128, 2, D], F32)
        l2b_sb, _ = sb("l2b_sb", [128, 2, D], F32)

        def rezero(v):
            S.inc(v.memset(zsq_sb[:, :], 0.0), "sV")

        V_(rezero)
        rz_tok = S.cur("sV")

        def ffna_fill(z):
            S.wait(z, "z", rz_tok)
            for b in range(NB):
                for tp in range(16):
                    dma(z, "z", "dZ", ffna_d[b][tp * 128:(tp + 1) * 128, :],
                        zsq_sb[:, :])

        C_(ffna_fill)
        ffna_fill_tok = S.cur("dZ")

        NIT = NE * NB
        xgt_free = [None, None, None, None]
        xnat_free = [None, None]
        xin_scat_tok = [None] * NIT
        xinT_tok = [None] * NIT
        meta_tok = [None] * NIT
        xinT_used = [None] * NIT
        meta_used = [None] * NIT
        xnat_ldt = [None] * NIT
        trload_done = [None] * 4
        hT_done = [None] * NIT
        w1p_tok = {}
        w1p_used = {}
        w2_tok = [None] * (NE + 1)
        w2_used = [None] * NE
        ysb_free = [None]
        ln2_ok = False

        def emit_xin_scatter(it):
            e, bb = it // NB, it % NB
            slot = it % 4
            for tp in range(16):
                gpar = tp % 4

                xch = f"dYx{tp % 4}"

                def fl(y, bb=bb, tp=tp, gpar=gpar, xch=xch):
                    S.wait(y, "y", tok_xgd_x1[bb])
                    S.wait(y, "y", tok_xgd_meta[bb])
                    S.wait(y, "y", xgt_free[gpar])
                    dma(y, "y", xch, xgt_sb[:, gpar, :],
                        xgd_d[bb, tp * 128:(tp + 1) * 128, :])

                Y_(fl)
                ldtok = S.cur(xch)

                ch = f"dGs{gpar}"

                def fs(g, e=e, bb=bb, slot=slot, it=it, tp=tp, gpar=gpar,
                       ldtok=ldtok, ch=ch):
                    S.wait(g, "g", posu_tok)
                    S.wait(g, "g", ldtok)
                    if it >= 4 and tp < 4:
                        S.wait(g, "g", trload_done[slot])
                        S.wait(g, "g", meta_tok[it - 4])
                    S.wait(g, "g", S.cur(ch))
                    inst = g.indirect_dma_start(
                        out=xin_d[slot][:, :],
                        out_offset=IndirectOffsetOnAxis(
                            posu_sb[:, tp, bb * 32 + e:bb * 32 + e + 1], 0),
                        in_=xgt_sb[:, gpar, :],
                        in_offset=None,
                        bounds_check=bc_reg[0],
                        oob_is_err=False)
                    S.inc(inst, ch, 16)

                G_(fs)
                xgt_free[gpar] = S.cur(ch)
            xin_scat_tok[it] = [S.cur(f"dGs{i}") for i in range(4)]

        def emit_xnat_loads(it):
            slot = it % 4
            ldt = []
            for sp in range(2):
                npar = (it * 4 + sp) % 2
                nfree = xnat_free[npar]

                def fl(y, slot=slot, it=it, sp=sp, npar=npar, nfree=nfree):
                    for tk in xin_scat_tok[it]:
                        S.wait(y, "y", tk)
                    S.wait(y, "y", nfree)
                    dma(y, "y", "dY", xnat_sb[:, npar, :],
                        xin_d[slot][sp * 128:(sp + 1) * 128, 0:1024].bitcast(BF16))

                Y_(fl)
                ldt.append(S.cur("dY"))
            xnat_ldt[it] = ldt

        def emit_xin_tr(it):
            xpar = it % 2
            slot = it % 4
            trx = [S.reserve("sT") for _ in range(4)]
            evx = [S.reserve("sV") for _ in range(4)]
            ldt = list(xnat_ldt[it])
            for sp in range(2, 4):
                npar = (it * 4 + sp) % 2

                def fl(y, slot=slot, it=it, sp=sp, npar=npar, w=trx[sp - 2]):
                    S.wait(y, "y", w)
                    dma(y, "y", "dY", xnat_sb[:, npar, :],
                        xin_d[slot][sp * 128:(sp + 1) * 128, 0:1024].bitcast(BF16))

                Y_(fl)
                ldt.append(S.cur("dY"))
            trload_done[slot] = ldt[-1]
            psT_at = []
            for j in range(4):
                psT_at.append(psT_free[j % 2])
                psT_free[j % 2] = evx[j]
            xnat_free[0] = trx[2]
            xnat_free[1] = trx[3]

            def ftr(t, it=it, ldt=tuple(ldt), trx=tuple(trx),
                    psT_at=tuple(psT_at)):
                for sp in range(4):
                    npar = (it * 4 + sp) % 2
                    S.wait(t, "t", ldt[sp])
                    S.wait(t, "t", psT_at[sp])
                    for c in range(8):
                        inst = t.transpose(
                            psT[sp % 2][:, c * 128:(c + 1) * 128],
                            xnat_sb[:, npar, c * 128:(c + 1) * 128],
                            idb_sb[:, :])
                    S.attach(inst, "sT")

            TB_(ftr)

            def fev(v, it=it, xpar=xpar, trx=tuple(trx)):
                if it >= 2:
                    S.wait(v, "v", xinT_used[it - 2])
                for sp in range(4):
                    S.wait(v, "v", trx[sp])
                    inst = v.tensor_copy(
                        xinT_sb[:, xpar, :, sp * 128:(sp + 1) * 128],
                        psT[sp % 2][:, :].rearrange("p (a q) -> p a q", a=8))
                    S.attach(inst, "sV")

            VB_(fev)
            xinT_tok[it] = evx[3]

        def emit_xin_meta(it):
            slot = it % 4
            xpar = it % 2

            def fm(y, slot=slot, xpar=xpar, it=it):
                for tk in xin_scat_tok[it]:
                    S.wait(y, "y", tk)
                if it >= 2:
                    S.wait(y, "y", meta_used[it - 2])
                dma(y, "y", "dZ", meta_sb[:, xpar, :, :],
                    xin_d[slot][:, 1024:1048].rearrange("(j p) m -> p j m", p=128))

            Y_(fm)
            meta_tok[it] = S.cur("dZ")

        def emit_w1_piece(gpi):
            it, pf = gpi // 4, gpi % 4
            e = it // NB
            slot = gpi % 2

            wch = f"dYw{pf}"

            def f(g, e=e, pf=pf, slot=slot, gpi=gpi, wch=wch):
                for tk in ffn_barrier:
                    S.wait(g, "g", tk)
                if gpi - 2 in w1p_used and w1p_used[gpi - 2] is not None:
                    g.wait_ge(S.h["sT"], w1p_used[gpi - 2][1])
                dma(g, "g", wch, w1r_sb[:, slot, :, :],
                    w1_d[e, :, pf * 1024:(pf + 1) * 1024].rearrange(
                        "(c p) f -> p c f", p=128))

            G_(f)
            w1p_tok[gpi] = S.cur(wch)

        def emit_w2_load(e):
            def f(g, e=e):
                for tk in ffn_barrier:
                    S.wait(g, "g", tk)
                if e > 0:
                    g.wait_ge(S.h["sT"], w2_used[e - 1][1])
                for piece in range(4):
                    dma(g, "g", f"dYv{piece}",
                        w2_sb[:, piece * 8:(piece + 1) * 8, :],
                        w2_d[e, piece * 1024:(piece + 1) * 1024, :].rearrange(
                            "(c p) d -> p c d", p=128))

            G_(f)
            w2_tok[e] = [S.cur("dYv0"), S.cur("dYv1"),
                         S.cur("dYv2"), S.cur("dYv3")]

        emit_w2_load(0)
        for gpi in range(2):
            emit_w1_piece(gpi)
        emit_xin_scatter(0)
        emit_xin_meta(0)
        emit_xin_scatter(1)
        emit_xin_meta(1)
        emit_xnat_loads(0)
        emit_xin_tr(0)

        prev_scat = None
        last_scat_b = [None, None]
        next_w1 = 2
        for it in range(NIT):
            e, bb = it // NB, it % NB
            xpar = it % 2
            hmm = [S.reserve("sT") for _ in range(32)]
            hge = [S.reserve("sS") for _ in range(32)]
            hent = list(bank_free) + [hge[j - 4] for j in range(4, 32)]
            bank_free = [hge[28], hge[29], hge[30], hge[31]]
            for pf in range(4):
                w1p_used[it * 4 + pf] = hmm[pf * 8 + 7]
            while next_w1 <= it * 4 + 3:
                emit_w1_piece(next_w1)
                next_w1 += 1

            def h_sec(t, it=it, xpar=xpar, hent=tuple(hent)):
                S.wait(t, "t", xinT_tok[it])
                for j in range(32):
                    pf, fc = j // 8, j % 8
                    gpi = it * 4 + pf
                    S.wait(t, "t", w1p_tok[gpi])
                    S.wait(t, "t", hent[j])
                    for k in range(8):
                        inst = t.matmul(
                            psE[j % 4][:, :],
                            w1r_sb[:, gpi % 2, k, fc * 128:(fc + 1) * 128],
                            xinT_sb[:, xpar, k, :],
                            start=(k == 0), stop=(k == 7))
                    S.attach(inst, "sT")

            TB_(h_sec)

            def g_sec(c, hmm=tuple(hmm)):
                for j in range(32):
                    S.wait(c, "c", hmm[j])
                    inst = c.activation(out=hTg_sb[:, j, :],
                                        in_=psE[j % 4][:, :], func=AF.Gelu)
                    S.attach(inst, "sS")

            CB_(g_sec)
            while next_w1 <= min(it * 4 + 5, NIT * 4 - 1):
                emit_w1_piece(next_w1)
                next_w1 += 1
            xinT_used[it] = hmm[31]
            hT_done[it] = hge[31]
            scat_toks = [S.reserve("dGf", 16) for _ in range(4)]
            meta_used[it] = scat_toks[3]
            if it + 1 < NIT:
                emit_xnat_loads(it + 1)
            if it + 2 < NIT:
                emit_xin_scatter(it + 2)
                emit_xin_meta(it + 2)

            ymm = [S.reserve("sT") for _ in range(8)]
            ysc = [S.reserve("sV", 2) for _ in range(4)]
            yent = [pv_free[0], pv_free[1]] + [ysc[(j - 2) // 2] for j in range(2, 8)]
            pv_free[0], pv_free[1] = ysc[3], ysc[3]

            def y_sec(t, it=it, e=e, yent=tuple(yent)):
                S.wait(t, "t", hT_done[it])
                for tk in w2_tok[e]:
                    S.wait(t, "t", tk)
                for j in range(8):
                    sp, dc = j // 2, j % 2
                    S.wait(t, "t", yent[j])
                    for fc in range(32):
                        inst = t.matmul(pb45[j % 2][:, :],
                                        hTg_sb[:, fc, sp * 128:(sp + 1) * 128],
                                        w2_sb[:, fc, dc * 512:(dc + 1) * 512],
                                        start=(fc == 0), stop=(fc == 31))
                    S.attach(inst, "sT")

            TB_(y_sec)

            def yv_sec(v, it=it, e=e, xpar=xpar, ymm=tuple(ymm), ysc=tuple(ysc),
                       yfree=ysb_free[0], scat_toks=tuple(scat_toks)):
                S.wait(v, "v", meta_tok[it])
                for sp in range(4):
                    g_ap = meta_sb[:, xpar, sp, 2 * e:2 * e + 2].bitcast(F32)
                    if sp == 0:
                        S.wait(v, "v", yfree)
                    else:
                        S.wait(v, "v", scat_toks[sp - 1])
                    for dc in range(2):
                        S.wait(v, "v", ymm[sp * 2 + dc])
                        inst = v.tensor_scalar(
                            out=y_sb[:, 0, dc * 512:(dc + 1) * 512],
                            in0=pb45[dc][:, :],
                            scalar1=g_ap, scalar2=None, op0=ALU.mult)
                        S.attach(inst, "sV")

            VB_(yv_sec)

            def y_scat_sec(g, it=it, e=e, bb=bb, xpar=xpar, ysc=tuple(ysc),
                           pscat=prev_scat):
                S.wait(g, "g", ffna_fill_tok)
                if pscat is not None:
                    S.wait(g, "g", pscat)
                for sp in range(4):
                    S.wait(g, "g", ysc[sp])
                    inst = g.indirect_dma_start(
                        out=ffna_d[bb][:, :],
                        out_offset=IndirectOffsetOnAxis(
                            meta_sb[:, xpar, sp, 16:18].bitcast(U32), 0),
                        in_=y_sb[:, 0, :],
                        in_offset=None,
                        compute_op=ALU.add)
                    S.attach(inst, "dGf", 16)
                    if sp < 3:
                        g.wait_ge(S.h["dGf"], scat_toks[sp][1])

            G_(y_scat_sec)
            prev_scat = scat_toks[3]
            ysb_free[0] = scat_toks[3]
            last_scat_b[bb] = scat_toks[3]
            w2_used[e] = ymm[7]
            if it + 1 < NIT:
                emit_xin_tr(it + 1)
            if bb == NB - 1 and e + 1 < NE:
                emit_w2_load(e + 1)
        all_scat = prev_scat

        l2_free = [None, None]
        l2b_free = [None, None]
        for bb in range(NB):
            for pk in range(8):
                ldtok = [None, None]
                for rpar in range(2):
                    tp = 2 * pk + rpar
                    lch = f"dY2{(pk % 2) * 2 + rpar}"

                    def l2_load(y, bb=bb, tp=tp, rpar=rpar, lch=lch):
                        S.wait(y, "y", last_scat_b[bb])
                        S.wait(y, "y", pos_done_tok)
                        S.wait(y, "y", l2_free[rpar])
                        S.wait(y, "y", l2b_free[rpar])
                        dma(y, "y", lch, l2a_sb[:, rpar, :],
                            x1sp_d[bb, tp * 128:(tp + 1) * 128, :])
                        dma(y, "y", lch, l2b_sb[:, rpar, :],
                            ffna_d[bb][tp * 128:(tp + 1) * 128, :])

                    Y_(l2_load)
                    ldtok[rpar] = S.cur(lch)

                base = 1 + 14 * (pk % 2)
                t0 = l2a_sb[:, 0, :]
                t1 = l2a_sb[:, 1, :]

                def l2_st1(v, ldtok=tuple(ldtok), base=base):
                    S.wait(v, "v", ldtok[0])
                    i = v.scalar_tensor_tensor(out=t0, in0=l2b_sb[:, 0, :],
                                               scalar=1.0, in1=t0,
                                               op0=ALU.bypass, op1=ALU.add,
                                               accum_out=stat_sb[:, base:base + 1])
                    a0 = S.inc(i, "sV", 1)
                    S.wait(v, "v", ldtok[1])
                    i = v.scalar_tensor_tensor(out=t1, in0=l2b_sb[:, 1, :],
                                               scalar=1.0, in1=t1,
                                               op0=ALU.bypass, op1=ALU.add,
                                               accum_out=stat_sb[:, base + 1:base + 2])
                    a1 = S.inc(i, "sV", 1)
                    S.wait(v, "v", a0)
                    i = v.scalar_tensor_tensor(out=zsq_sb[:, :], in0=t0,
                                               scalar=1.0 / D, in1=t0,
                                               op0=ALU.mult, op1=ALU.mult,
                                               accum_out=stat_sb[:, base + 2:base + 3])
                    b0 = S.inc(i, "sV", 1)
                    S.wait(v, "v", a1)
                    i = v.tensor_scalar(out=stat_sb[:, base:base + 2],
                                        in0=stat_sb[:, base:base + 2],
                                        scalar1=1.0 / D, scalar2=None, op0=ALU.mult)
                    c = S.inc(i, "sV", 1)
                    i = v.scalar_tensor_tensor(out=l2b_sb[:, 0, :], in0=t1,
                                               scalar=1.0 / D, in1=t1,
                                               op0=ALU.mult, op1=ALU.mult,
                                               accum_out=stat_sb[:, base + 3:base + 4])
                    b1 = S.inc(i, "sV", 1)
                    S.wait(v, "v", c)
                    i = v.tensor_tensor(out=stat_sb[:, base + 4:base + 6],
                                        in0=stat_sb[:, base:base + 2],
                                        in1=stat_sb[:, base:base + 2], op=ALU.mult)
                    dtk = S.inc(i, "sV", 1)
                    S.wait(v, "v", b0)
                    S.wait(v, "v", b1)
                    S.wait(v, "v", dtk)
                    i = v.scalar_tensor_tensor(out=stat_sb[:, base + 6:base + 8],
                                               in0=stat_sb[:, base + 4:base + 6],
                                               scalar=1e-5, in1=stat_sb[:, base + 2:base + 4],
                                               op0=ALU.subtract, op1=ALU.subtract,
                                               accum_out=None)
                    e = S.inc(i, "sV", 1)
                    S.wait(v, "v", e)
                    i = v.reciprocal(stat_sb[:, base + 8:base + 10],
                                     stat_sb[:, base + 6:base + 8])
                    S.inc(i, "sV", 1)

                V_(l2_st1)
                vtok = S.cur("sV")
                l2b_free[0] = vtok
                l2b_free[1] = vtok

                def l2_rs(c, base=base, vtok=vtok):
                    S.wait(c, "c", vtok)
                    i = c.activation(out=stat_sb[:, base + 10:base + 12],
                                     in_=stat_sb[:, base + 8:base + 10],
                                     func=AF.Sqrt, scale=-1.0)
                    S.inc(i, "sS", 1)

                C_(l2_rs)
                stok = S.cur("sS")

                def l2_ap(v, base=base, stok=stok):
                    S.wait(v, "v", stok)
                    i = v.tensor_scalar(out=t0, in0=t0,
                                        scalar1=stat_sb[:, base:base + 1],
                                        scalar2=stat_sb[:, base + 10:base + 11],
                                        op0=ALU.subtract, op1=ALU.mult)
                    S.inc(i, "sV", 1)
                    i = v.tensor_scalar(out=t1, in0=t1,
                                        scalar1=stat_sb[:, base + 1:base + 2],
                                        scalar2=stat_sb[:, base + 11:base + 12],
                                        op0=ALU.subtract, op1=ALU.mult)
                    S.inc(i, "sV", 1)

                V_(l2_ap)
                atok = S.cur("sV")

                def o_store(y, bb=bb, pk=pk, atok=atok):
                    S.wait(y, "y", atok)
                    dma(y, "y", "dO", out_d[bb, 2 * pk * 128:(2 * pk + 1) * 128, :],
                        l2a_sb[:, 0, :])
                    dma(y, "y", "dO1",
                        out_d[bb, (2 * pk + 1) * 128:(2 * pk + 2) * 128, :],
                        l2a_sb[:, 1, :])

                Y_(o_store)
                l2_free[0] = S.cur("dO")
                l2_free[1] = S.cur("dO1")

        def fin(y):
            S.wait(y, "y", S.cur("dO"))
            S.wait(y, "y", S.cur("dO1"))

        Y_(fin)

    return nc


def kernel(L0_attn_x, L0_attn_Wq, L0_attn_Wkv_down, L0_attn_Wk_up, L0_attn_Wv_up,
           L0_attn_Wo, L0_ffn_gate_w, L0_ffn_w1, L0_ffn_w2):
    x = np.ascontiguousarray(L0_attn_x)
    wqT = np.ascontiguousarray(np.asarray(L0_attn_Wq).T)
    wkvT = np.ascontiguousarray(np.asarray(L0_attn_Wkv_down).T)
    wkT = np.ascontiguousarray(np.asarray(L0_attn_Wk_up).T)
    wvT = np.ascontiguousarray(np.asarray(L0_attn_Wv_up).T)
    woT = np.ascontiguousarray(np.asarray(L0_attn_Wo).T)
    gwT = np.ascontiguousarray(np.asarray(L0_ffn_gate_w).T)
    w1 = np.ascontiguousarray(L0_ffn_w1)
    w2 = np.ascontiguousarray(L0_ffn_w2)
    ident = np.eye(128, dtype=np.float32)

    nc = build()
    in_maps = []
    for c in range(8):
        in_maps.append({
            "x": np.ascontiguousarray(x[2 * c:2 * c + 2]),
            "wqT": wqT, "wkvT": wkvT, "wkT": wkT, "wvT": wvT, "woT": woT,
            "gwT": gwT, "w1": w1, "w2": w2, "ident": ident,
        })
    res = run_bass_kernel_spmd(nc, in_maps, core_ids=list(range(8)))
    out = np.concatenate([r["out"] for r in res.results], axis=0)
    return out.reshape(16, T, D).astype(np.float32)
